# version 12 (final)
# speedup vs baseline: 1.0941x; 1.0122x over previous
import contextlib
import numpy as np
import ml_dtypes
import concourse.bass as bass
import concourse.mybir as mybir
from concourse.bass_utils import run_bass_kernel_spmd

F32 = mybir.dt.float32
BF16 = mybir.dt.bfloat16
AF = mybir.ActivationFunctionType
ALU = mybir.AluOpType
AX = mybir.AxisListType

T = 2048
C = 1024
NT = 16
NCH = 8
HEADS = 16
HD = 64
LN_EPS = 1e-5
GN0_EPS = 0.00064
GN1_EPS = 1e-5
RMS_EPS = 1.1920929e-07
NCORES = 8

ENGINES = ("sync", "gpsimd", "tensor", "vector", "scalar")



class Op:
    __slots__ = ("engine", "fn", "sem", "inc", "cum", "deps", "need_inc", "tag")

    def __init__(self, engine, fn, sem, inc, tag):
        self.engine = engine
        self.fn = fn
        self.sem = sem
        self.inc = inc
        self.cum = 0
        self.deps = []
        self.need_inc = False
        self.tag = tag


class Prog:
    def __init__(self):
        self.ops = []
        self.last_writer = {}
        self.readers = {}

    def emit(self, engine, fn, reads=(), writes=(), dma=False, tag=""):
        sem = ("d" + engine) if dma else engine
        op = Op(engine, fn, sem, 16 if dma else 1, tag)
        if engine == "scalar" and not dma:
            reads = list(reads) + [CF_KEY]
        psum_r = [k for k in reads if k[0] in ("PSA", "PSB", "PSC", "PST")]
        if psum_r:
            reads = [k for k in reads if k[0] not in ("PSA", "PSB", "PSC", "PST")]
            writes = list(writes) + psum_r
        deps = []
        for k in reads:
            w = self.last_writer.get(k)
            if w is not None:
                deps.append(w)
        for k in writes:
            w = self.last_writer.get(k)
            if w is not None:
                deps.append(w)
            deps.extend(self.readers.get(k, ()))
        seen = set()
        for d in deps:
            if d is op or id(d) in seen:
                continue
            seen.add(id(d))
            if (d.engine == op.engine and d.sem == op.sem
                    and op.engine == "tensor" and op.inc != 16):
                continue
            op.deps.append(d)
        for k in reads:
            self.readers.setdefault(k, []).append(op)
        for k in writes:
            self.last_writer[k] = op
            self.readers[k] = []
        self.ops.append(op)
        return op

    def finalize(self, block, sems):
        NPOOL = 12
        pos = {}
        for i, o in enumerate(self.ops):
            pos[id(o)] = i
        dma_idx = {}
        last_on_sem = {}
        consumers = {id(o): [] for o in self.ops}
        for op in self.ops:
            for d in op.deps:
                consumers[id(d)].append(op)
        for op in self.ops:
            if op.inc != 16:
                continue
            qi = dma_idx.get(op.sem, 0)
            dma_idx[op.sem] = qi + 1
            op.sem = f"{op.sem}{qi % NPOOL}"
            prev = last_on_sem.get(op.sem)
            if prev is not None:
                have = {id(d) for d in op.deps}
                for c in consumers[id(prev)]:
                    if c is op:
                        continue
                    if pos[id(c)] < pos[id(op)]:
                        if id(c) not in have and c.engine != op.engine:
                            op.deps.append(c)
                            have.add(id(c))
                    else:
                        if id(op) not in {id(d) for d in c.deps}:
                            c.deps.append(op)
            last_on_sem[op.sem] = op
        for op in self.ops:
            if op.inc == 16:
                op.need_inc = True
            for d in op.deps:
                d.need_inc = True
        cums = {}
        for op in self.ops:
            if op.need_inc:
                cums[op.sem] = cums.get(op.sem, 0) + op.inc
                op.cum = cums[op.sem]
        per_engine = {e: [] for e in ENGINES}
        for op in self.ops:
            per_engine[op.engine].append(op)

        def make_body(eng_name):
            ops = per_engine[eng_name]

            def run(eng):
                waited = {}
                for op in ops:
                    wants = {}
                    for d in op.deps:
                        if d.need_inc:
                            v = wants.get(d.sem, 0)
                            if d.cum > v:
                                wants[d.sem] = d.cum
                    for s, v in wants.items():
                        if waited.get(s, 0) < v:
                            eng.wait_ge(sems[s], v)
                            waited[s] = v
                    ins = op.fn(eng)
                    if op.need_inc:
                        ins.then_inc(sems[op.sem], op.inc)
                if eng_name == "sync":
                    for s, v in cums.items():
                        if s[0] == "d" and waited.get(s, 0) < v:
                            eng.wait_ge(sems[s], v)
            return run

        for e in ENGINES:
            if per_engine[e] or e == "sync":
                getattr(block, e)(make_body(e))



VEC_ROWS = [
    "L0_attn_time_maa_x", "L0_attn_time_maa_w", "L0_attn_time_maa_k",
    "L0_attn_time_maa_v", "L0_attn_time_maa_r", "L0_attn_time_decay",
    "L0_attn_ln_w", "L0_attn_ln_b", "L0_attn_time_faaaa",
    "L0_ffn_bq", "L0_ffn_bk", "L0_ffn_bv", "L0_ffn_bo", "L0_ffn_br",
    "L1_attn_gn_weight", "L1_attn_gn_bias",
    "L1_ffn_n1_w", "L1_ffn_n2_w", "L1_ffn_qn_w_t", "L1_ffn_kn_w_t",
]
VI = {n: i for i, n in enumerate(VEC_ROWS)}
NV = len(VEC_ROWS)


def _pad_to_c(v):
    v = np.asarray(v, dtype=np.float32).reshape(-1)
    if v.size < C:
        v = np.pad(v, (0, C - v.size))
    return v


def make_host_aux(inputs):
    rows = []
    for name in VEC_ROWS:
        if name == "L0_attn_time_faaaa":
            rows.append(np.asarray(inputs["L0_attn_time_faaaa"]).reshape(-1))
        elif name == "L1_ffn_qn_w_t":
            rows.append(np.tile(np.asarray(inputs["L1_ffn_qn_w"]).reshape(-1), HEADS))
        elif name == "L1_ffn_kn_w_t":
            rows.append(np.tile(np.asarray(inputs["L1_ffn_kn_w"]).reshape(-1), HEADS))
        else:
            rows.append(_pad_to_c(inputs[name]))
    vecs = np.stack([np.asarray(r, np.float32) for r in rows])
    vecs_pt = np.ascontiguousarray(vecs.reshape(NV, NCH, 128).transpose(1, 2, 0))
    w2 = np.asarray(inputs["L0_attn_time_maa_w2"], np.float32)
    w2_lhsT = np.zeros((128, 4 * C), np.float32)
    for f in range(4):
        w2_lhsT[f * 32:(f + 1) * 32, f * C:(f + 1) * C] = w2[f]
    return {"vecs_pt": vecs_pt, "w2_lhsT": w2_lhsT}


def make_host_consts():
    gammas = 1.0 - 2.0 ** (-5.0 - np.arange(HEADS, dtype=np.float64))
    i128 = np.eye(128, dtype=np.float32)
    utri_strict = np.triu(np.ones((128, 128), np.float32), 1)
    utri_incl = np.triu(np.ones((128, 128), np.float32), 0)
    negu129 = np.zeros((128, 129), np.float32)
    for n in range(129):
        negu129[:n, n] = -1.0
    ones128 = np.ones((128, 512), np.float32)
    jj = np.arange(128)
    retm = np.zeros((HEADS, 128, 128), np.float64)
    for h in range(HEADS):
        retm[h] = (jj[None, :] >= jj[:, None]) * (gammas[h] ** (-jj[:, None].astype(np.float64)))
    rep = (jj[:, None] % 64 == (np.arange(256)[None, :] % 64)).astype(np.float32)
    sel4 = np.zeros((4, 256), np.float32)
    sel4[np.arange(256) // 64, np.arange(256)] = 1.0
    e16 = np.zeros((16, C), np.float32)
    e16[np.arange(C) // 64, np.arange(C)] = 1.0

    cb = np.zeros((128, 4624), np.float32)
    cofs = {}
    pos = [0]

    def put(name, arr):
        a = np.asarray(arr, np.float32)
        cb[: a.shape[0], pos[0]: pos[0] + a.shape[1]] = a
        cofs[name] = (pos[0], a.shape[1])
        pos[0] += a.shape[1]

    put("I128", i128)
    put("UTS", utri_strict)
    put("UTI", utri_incl)
    put("NEGU", negu129)
    put("ONES", ones128)
    put("RETM", np.concatenate([retm[h] for h in range(HEADS)], axis=1))
    put("REP", rep)
    put("SEL4", sel4)
    put("E16", e16)
    assert pos[0] <= 4624, pos[0]

    cf = np.zeros((128, 1536), np.float32)
    fofs = {}
    ll = np.arange(128, dtype=np.float64)
    cf[:, 0:128] = i128
    fofs["I128F"] = (0, 128)
    cf[:, 128:144] = np.stack([gammas[h] ** ll / 8.0 for h in range(HEADS)], axis=1)
    fofs["OSC"] = (128, 16)
    cf[:, 144:160] = np.stack([gammas[h] ** (127 - ll) for h in range(HEADS)], axis=1)
    fofs["KSC"] = (144, 16)
    cf[:, 160:161] = 1.0
    fofs["ONESF"] = (160, 1)
    cf[:16, 176:176 + C] = e16
    fofs["E16F"] = (176, C)
    konst_vals = [1e-5, 0.00064, 1.1920929e-07, -0.5, -1.0, 1.0 / 64, 1.0 / 1024,
                  1.0 / 131072.0, 0.5, 2.0, 0.125, -0.125, 1.0 / 8192.0, 1e-9,
                  -1e30, 1e-08]
    kbase = 176 + C
    kmap = {}
    for i, v in enumerate(konst_vals):
        cf[:, kbase + i] = v
        kmap[v] = kbase + i
    return {"cb": cb, "cb_ofs": cofs, "cf": cf, "cf_ofs": fofs, "gammas": gammas,
            "kmap": kmap}


_CONSTS = make_host_consts()



BAR_COLS = 59808
FAR_COLS = 18752


class Buf:

    def __init__(self, tensor, arena, lo, hi):
        self.t = tensor
        self.arena = arena
        self.lo = lo
        self.hi = hi

    def ap(self, cols=None, parts=None, rearr=None, **kw):
        lo = self.lo if cols is None else self.lo + cols[0]
        hi = self.hi if cols is None else self.lo + cols[1]
        if parts is None:
            a = self.t[:, lo:hi]
        else:
            a = self.t[parts[0]:parts[1], lo:hi]
        if rearr is not None:
            a = a.rearrange(rearr, **kw)
        return a

    def keys(self, cols=None):
        lo = self.lo if cols is None else self.lo + cols[0]
        hi = self.hi if cols is None else self.lo + cols[1]
        return [(self.arena, g) for g in range(lo // 1024, (hi + 1023) // 1024)]


DEBUG = False


class Ctx:
    def __init__(self, nc, p, stack):
        self.nc = nc
        self.p = p
        self.stack = stack
        self.t = {}
        self.sems = None
        self.dbg_map = {}
        self.dbg_slot = 0

    def dram(self, name, shape):
        if name not in self.t:
            self.t[name] = self.nc.dram_tensor(name, list(shape), F32, kind="ExternalInput")
        return self.t[name]

    def bar(self, lo, hi):
        return Buf(self.t["BAR"], "BAR", lo, hi)

    def far(self, lo, hi):
        return Buf(self.t["FAR"], "FAR", lo, hi)


def build_ctx(stack):
    nc = bass.Bass("TRN2", target_bir_lowering=False)
    p = Prog()
    ctx = Ctx(nc, p, stack)
    t = ctx.t
    t["xin"] = nc.dram_tensor("xin", [T, C], F32, kind="ExternalInput")
    if DEBUG:
        t["dbgout"] = nc.dram_tensor("dbgout", [128, 48 * 2048], F32, kind="ExternalOutput")
    t["xout"] = nc.dram_tensor("xout", [T, C], F32, kind="ExternalOutput")
    t["vecs_pt"] = nc.dram_tensor("vecs_pt", [NCH, 128, NV], F32, kind="ExternalInput")
    t["cbd"] = nc.dram_tensor("cbd", [128, 4624], F32, kind="ExternalInput")
    t["cfd"] = nc.dram_tensor("cfd", [128, 1536], F32, kind="ExternalInput")
    t["BAR"] = stack.enter_context(nc.sbuf_tensor("BAR", [128, BAR_COLS], BF16))
    t["FAR"] = stack.enter_context(nc.sbuf_tensor("FAR", [128, FAR_COLS], F32))
    t["CB"] = stack.enter_context(nc.sbuf_tensor("CBs", [128, 4624], BF16))
    t["CF"] = stack.enter_context(nc.sbuf_tensor("CFs", [128, 1536], F32))
    t["VEC"] = stack.enter_context(nc.sbuf_tensor("VECs", [128, NCH * NV], F32))
    t["SMALL"] = stack.enter_context(nc.sbuf_tensor("SMALLs", [128, 512], F32))
    t["PSA"] = stack.enter_context(nc.psum_tensor("PSA", [128, 2048], F32))
    t["PSB"] = stack.enter_context(nc.psum_tensor("PSB", [128, 1024], F32))
    t["PSC"] = stack.enter_context(nc.psum_tensor("PSC", [128, 512], F32))
    t["PST"] = stack.enter_context(nc.psum_tensor("PST", [128, 1024], BF16))
    for v, col in _CONSTS["kmap"].items():
        nc.const_aps.register(F32, v, t["CF"][:, col:col + 1])
    sems = {}
    for s in ("tensor", "vector", "scalar", "gpsimd"):
        sems[s] = stack.enter_context(nc.semaphore("sem_" + s))
    for q in ("dgpsimd", "dsync", "dscalar"):
        for i in range(12):
            sems[f"{q}{i}"] = stack.enter_context(nc.semaphore(f"sem_{q}{i}"))
    ctx.sems = sems
    return ctx


CB_KEY = ("CB", 0)
CF_KEY = ("CF", 0)
VEC_KEY = ("VEC", 0)


def psk(name, lo, hi):
    g = 512
    return [(name, b) for b in range(lo // g, (hi + g - 1) // g)]


def cb_ap(ctx, name, cols=None, parts=None):
    o, sz = _CONSTS["cb_ofs"][name]
    lo = o if cols is None else o + cols[0]
    hi = (o + sz) if cols is None else o + cols[1]
    tt = ctx.t["CB"]
    return tt[:, lo:hi] if parts is None else tt[parts[0]:parts[1], lo:hi]


def cf_ap(ctx, name, cols=None, parts=None):
    o, sz = _CONSTS["cf_ofs"][name]
    lo = o if cols is None else o + cols[0]
    hi = (o + sz) if cols is None else o + cols[1]
    tt = ctx.t["CF"]
    return tt[:, lo:hi] if parts is None else tt[parts[0]:parts[1], lo:hi]


def vec_col(ctx, name, ci, parts=None):
    tt = ctx.t["VEC"]
    lo = ci * NV + VI[name]
    return (tt[:, lo:lo + 1] if parts is None
            else tt[parts[0]:parts[1], lo:lo + 1])


def dbg(ctx, name, ap, reads, cast=False):
    if not DEBUG:
        return
    slot = ctx.dbg_slot
    ctx.dbg_slot += 1
    assert slot < 48
    ctx.dbg_map[name] = slot
    dst = ctx.t["dbgout"][:, slot * 2048: slot * 2048 + 2048]
    nparts = ap.ap[0][1]
    ncols = 1
    for d in ap.ap[1:]:
        ncols *= d[1]
    dst = ctx.t["dbgout"][0:nparts, slot * 2048: slot * 2048 + ncols]
    eng = "gpsimd" if cast else "sync"
    ctx.p.emit(eng, lambda g, o=dst, i=ap: g.dma_start(o, i),
               reads=reads, writes=[("dbgout", slot)], dma=True, tag="dbg_" + name)


def load_commons(ctx):
    p, t = ctx.p, ctx.t
    p.emit("vector", lambda e, o=t["SMALL"][:, :]: e.memset(o, 0.0),
           writes=[("SMALL", i) for i in range(512)], tag="small0")
    p.emit("gpsimd", lambda g: g.dma_start(t["CB"][:, :], t["cbd"][:, :]),
           writes=[CB_KEY], dma=True, tag="ld_cb")
    p.emit("sync", lambda g: g.dma_start(t["CF"][:, :], t["cfd"][:, :]),
           writes=[CF_KEY], dma=True, tag="ld_cf")
    p.emit("sync", lambda g: g.dma_start(
        t["VEC"][:, :].rearrange("p (a k) -> p a k", k=NV),
        t["vecs_pt"].rearrange("a p k -> p a k")),
        writes=[VEC_KEY], dma=True, tag="ld_vec")


def mm(ctx, out_ap, lhsT_ap, rhs_ap, reads, writes, start=True, stop=True,
       tile_position=None, tag="mm"):
    ctx.p.emit("tensor",
               lambda e, o=out_ap, l=lhsT_ap, r=rhs_ap, st=start, sp=stop,
               tp=tile_position:
               e.matmul(o, l, r, start=st, stop=sp, tile_position=tp),
               reads=reads, writes=writes, tag=tag)


def tr(ctx, out_ap, in_ap, reads, writes, f32=False, psize=128, pbase=0, tag="tr"):
    if f32:
        ident = cf_ap(ctx, "I128F", cols=(pbase, pbase + psize),
                      parts=(pbase, pbase + psize))
    else:
        ident = cb_ap(ctx, "I128", cols=(pbase, pbase + psize),
                      parts=(pbase, pbase + psize))
    ctx.p.emit("tensor",
               lambda e, o=out_ap, i=in_ap, d=ident: e.transpose(o, i, d),
               reads=list(reads) + [CF_KEY if f32 else CB_KEY],
               writes=writes, tag=tag)



def load_x(ctx, xm):
    t = ctx.t
    src = t["xin"][:, :].rearrange("(a p) c -> p a c", p=128)
    ctx.p.emit("sync", lambda g: g.dma_start(
        xm.ap(rearr="p (a c) -> p a c", c=C), src),
        writes=xm.keys(), dma=True, tag="ld_x")


def store_x(ctx, xm):
    t = ctx.t
    dst = t["xout"][:, :].rearrange("(a p) c -> p a c", p=128)
    op = ctx.p.emit("sync", lambda g: g.dma_start(
        dst, xm.ap(rearr="p (a c) -> p a c", c=C)),
        reads=xm.keys(), writes=[("xout", 0)], dma=True, tag="st_x")
    op.need_inc = True
    ctx.p.emit("sync", lambda g: g.nop(), reads=[("xout", 0)], tag="st_wait")


def build_xt(ctx, xm, xt, xbs):
    p = ctx.p
    pst = ctx.t["PST"]
    for q in range(4):
        lo, hi = q * 4096, (q + 1) * 4096
        p.emit("vector", lambda e, o=xbs.ap((lo, hi)), i=xm.ap((lo, hi)):
               e.tensor_copy(o, i),
               reads=xm.keys((lo, hi)), writes=xbs.keys((lo, hi)), tag="xt_cast")
    for cj in range(NCH):
        for half in range(2):
            pk = pst_keys(ctx)
            for tq in range(8):
                ti = half * 8 + tq
                in_lo = ti * C + cj * 128
                tr(ctx, pst[:, tq * 128: (tq + 1) * 128],
                   xbs.ap((in_lo, in_lo + 128)),
                   reads=xbs.keys((in_lo, in_lo + 128)), writes=pk, tag="xt_tr")
            dst_lo = cj * T + half * 1024
            p.emit("vector", lambda e, o=xt.ap((dst_lo, dst_lo + 1024)),
                   i=pst[:, 0:1024]:
                   e.tensor_copy(o, i),
                   reads=pk, writes=xt.keys((dst_lo, dst_lo + 1024)), tag="xt_ev")


def pst_keys(ctx):
    return [("PST", 0), ("PST", 1), ("PST", 2), ("PST", 3)]



def stage_epilogue_tile(ctx, ti, psum_t, psum_col, psum_keys, xm, zf, sq, eps=LN_EPS):
    p = ctx.p
    sm = ctx.t["SMALL"]
    s0 = 0
    xk = xm.keys((ti * C, (ti + 1) * C))
    zk = zf.keys()
    K = lambda i: ("SMALL", i)
    p.emit("vector", lambda e, a=zf.ap(), x=xm.ap((ti * C, (ti + 1) * C)),
           s=psum_t[:, psum_col:psum_col + C], so=sm[:, s0:s0 + 1]:
           e.scalar_tensor_tensor(a, s, 1.0, x, ALU.mult, ALU.add, accum_out=so),
           reads=psum_keys + xk, writes=zk + [K(0)], tag=f"ep_add{ti}")
    p.emit("scalar", lambda e, o=sq.ap(), i=zf.ap(), so=sm[:, s0 + 1:s0 + 2]:
           e.activation(o, i, AF.Square, accum_out=so),
           reads=zk, writes=sq.keys() + [K(1)], tag=f"ep_sq{ti}")
    p.emit("vector", lambda e, o=sm[:, s0 + 2:s0 + 3], i=sm[:, s0:s0 + 1]:
           e.tensor_scalar_mul(o, i, 1.0 / C), reads=[K(0)], writes=[K(2)], tag="ep_mean")
    p.emit("vector", lambda e, o=sm[:, s0 + 3:s0 + 4], m=sm[:, s0 + 2:s0 + 3]:
           e.tensor_mul(o, m, m), reads=[K(2)], writes=[K(3)], tag="ep_m2")
    p.emit("vector", lambda e, o=sm[:, s0 + 4:s0 + 5], i=sm[:, s0 + 1:s0 + 2],
           m2=sm[:, s0 + 3:s0 + 4]:
           e.scalar_tensor_tensor(o, i, 1.0 / C, m2, ALU.mult, ALU.subtract),
           reads=[K(1), K(3)], writes=[K(4)], tag="ep_var")
    p.emit("scalar", lambda e, o=sm[:, s0 + 5:s0 + 6], i=sm[:, s0 + 4:s0 + 5]:
           e.activation(o, i, AF.Ln, bias=float(eps)),
           reads=[K(4)], writes=[K(5)], tag="ep_ln")
    p.emit("scalar", lambda e, o=sm[:, s0 + 6:s0 + 7], i=sm[:, s0 + 5:s0 + 6]:
           e.activation(o, i, AF.Exp, scale=-0.5),
           reads=[K(5)], writes=[K(6)], tag="ep_exp")
    p.emit("vector", lambda e, o=sm[:, s0 + 7:s0 + 8], m=sm[:, s0 + 2:s0 + 3],
           r=sm[:, s0 + 6:s0 + 7]:
           e.scalar_tensor_tensor(o, m, -1.0, r, ALU.mult, ALU.mult),
           reads=[K(2), K(6)], writes=[K(7)], tag="ep_b")
    p.emit("scalar", lambda e, o=xm.ap((ti * C, (ti + 1) * C)), i=zf.ap(),
           sc=sm[:, s0 + 6:s0 + 7], b=sm[:, s0 + 7:s0 + 8]:
           e.activation(o, i, AF.Identity, bias=b, scale=sc),
           reads=zk + [K(6), K(7)], writes=xk, tag=f"ep_app{ti}")


def transpose_weight(ctx, nat_buf, wt_buf, rows, cols, tag):
    p, t = ctx.p, ctx.t
    rk, ck = rows // 128, cols // 128
    for ri in range(rk):
        for cj in range(ck):
            in2 = bass.AP(t["BAR"], nat_buf.lo + ri * cols + cj * 128,
                          [[BAR_COLS, 128], [1, 128]])
            out2 = bass.AP(t["BAR"], wt_buf.lo + cj * rows + ri * 128,
                           [[BAR_COLS, 128], [1, 128]])
            p.emit("sync", lambda g, o=out2, i=in2: g.dma_start_transpose(o, i),
                   reads=nat_buf.keys((ri * cols + cj * 128, ri * cols + (cj + 1) * 128)),
                   writes=wt_buf.keys((cj * rows + ri * 128, cj * rows + (ri + 1) * 128)),
                   dma=True, tag=tag + "t")



def stage_l1a(ctx, xm):
    p, nc, t = ctx.p, ctx.nc, ctx.t
    psa, psb, psc, pst = t["PSA"], t["PSB"], t["PSC"], t["PST"]
    sm = t["SMALL"]
    gammas = _CONSTS["gammas"]

    for nm in ("L1_attn_W_q", "L1_attn_W_k", "L1_attn_W_v", "L1_attn_W_o"):
        ctx.dram(nm, (C, C))

    XT = ctx.bar(0, 16384)
    YNT = ctx.bar(16384, 32768)
    QT2 = ctx.bar(32768, 34816)
    KT2 = ctx.bar(34816, 36864)
    VT2 = ctx.bar(36864, 38912)
    VE = [ctx.bar(38912, 39952), ctx.bar(39952, 40992)]
    OH = ctx.bar(40992, 42016)
    AMB = ctx.bar(42016, 42144)
    KSB = ctx.bar(42144, 42208)
    KTOK = ctx.bar(42208, 43232)
    SB = ctx.bar(43232, 43296)
    OSQ = ctx.bar(43296, 44320)
    NATQ = [ctx.bar(44320 + i * 1024, 45344 + i * 1024) for i in range(2)]
    NATK = [ctx.bar(46368 + i * 1024, 47392 + i * 1024) for i in range(2)]
    NATV = [ctx.bar(48416 + i * 1024, 49440 + i * 1024) for i in range(2)]
    BCOL = ctx.bar(49440, 49448)
    BROWB = ctx.bar(49448, 50472)
    NQT = ctx.bar(50472, 51496)
    NKT = ctx.bar(51496, 52520)
    NVT = ctx.bar(52520, 53544)

    XBS = ctx.bar(16384, 32768)

    ZF = ctx.far(16384, 17408)
    SF = ctx.far(17408, 17472)
    STATF = ctx.far(17472, 17480)
    ALPHACOL = ctx.far(17480, 17488)
    MCOL = ctx.far(17488, 17489)
    RCOL = ctx.far(17489, 17490)
    MBCOL = ctx.far(17490, 17498)
    RBCOL = ctx.far(17498, 17506)
    BCOLF = ctx.far(17506, 17514)

    build_xt(ctx, xm, XT, XBS)
    dbg(ctx, "xt0", XT.ap((0, 2048)), XT.keys((0, 2048)), cast=True)

    for ph in range(8):
        par = ph % 2
        for wname, natb in (("L1_attn_W_q", NATQ[par]), ("L1_attn_W_k", NATK[par]),
                            ("L1_attn_W_v", NATV[par])):
            srcw = t[wname][ph * 128:(ph + 1) * 128, :]
            p.emit("gpsimd", lambda g, o=natb.ap(), i=srcw:
                   g.dma_start(o, i), writes=natb.keys(), dma=True, tag=f"l1a_wld{ph}")
        for natb, wtb in ((NATQ[par], NQT), (NATK[par], NKT), (NATV[par], NVT)):
            pk = pst_keys(ctx)
            for a in range(NCH):
                tr(ctx, pst[:, a * 128:(a + 1) * 128],
                   natb.ap((a * 128, (a + 1) * 128)),
                   reads=natb.keys(), writes=pk, tag="l1a_wtr")
            p.emit("vector", lambda e, o=wtb.ap(), i=pst[:, 0:1024]:
                   e.tensor_copy(o, i), reads=pk, writes=wtb.keys(), tag="l1a_wte")
        if ph == 0:
            dbg(ctx, "nqt", NQT.ap(), NQT.keys(), cast=True)
        for wtb, dstb in ((NQT, QT2), (NKT, KT2), (NVT, VT2)):
            for nchk in range(4):
                pkk = psk("PSA", nchk * 512, (nchk + 1) * 512)
                for a in range(NCH):
                    mm(ctx, psa[:, nchk * 512:(nchk + 1) * 512],
                       wtb.ap((a * 128, (a + 1) * 128)),
                       XT.ap((a * T + nchk * 512, a * T + (nchk + 1) * 512)),
                       reads=wtb.keys() + XT.keys((a * T + nchk * 512, a * T + (nchk + 1) * 512)),
                       writes=pkk, start=(a == 0), stop=(a == NCH - 1), tag="l1a_proj")
                p.emit("vector", lambda e, o=dstb.ap((nchk * 512, (nchk + 1) * 512)),
                       i=psa[:, nchk * 512:(nchk + 1) * 512]:
                       e.tensor_copy(o, i),
                       reads=pkk, writes=dstb.keys((nchk * 512, (nchk + 1) * 512)),
                       tag="l1a_pev")
        if ph == 0:
            dbg(ctx, "qt2", QT2.ap(), QT2.keys(), cast=True)
            dbg(ctx, "kt2", KT2.ap(), KT2.keys(), cast=True)
            dbg(ctx, "vt2", VT2.ap(), VT2.keys(), cast=True)

        for sub in range(2):
            h = 2 * ph + sub
            base = sub * 64
            g1 = float(gammas[h])
            g128 = float(gammas[h] ** 128)
            p.emit("vector", lambda e, o=bass.AP(t["BAR"], VE[sub].lo + 64,
                                                 [[BAR_COLS, 128], [65, 16], [1, 1]]):
                   e.memset(o, 1.0), writes=VE[sub].keys(), tag="l1a_ones")
            for half in range(2):
                pk = pst_keys(ctx)
                for o8 in range(8):
                    ti = half * 8 + o8
                    tr(ctx, pst[:, o8 * 64:(o8 + 1) * 64],
                       VT2.ap((ti * 128, (ti + 1) * 128), parts=(base, base + 64)),
                       reads=VT2.keys((ti * 128, (ti + 1) * 128)), writes=pk,
                       psize=64, pbase=base, tag="l1a_vt")
                p.emit("vector", lambda e,
                       o=bass.AP(t["BAR"], VE[sub].lo + half * 8 * 65,
                                 [[BAR_COLS, 128], [65, 8], [1, 64]]),
                       i=bass.AP(t["PST"], 0, [[1024, 128], [64, 8], [1, 64]]):
                       e.tensor_copy(o, i),
                       reads=pk, writes=VE[sub].keys(), tag="l1a_vev")
                for o8 in range(8):
                    ti = half * 8 + o8
                    tr(ctx, pst[:, o8 * 64:(o8 + 1) * 64],
                       KT2.ap((ti * 128, (ti + 1) * 128), parts=(base, base + 64)),
                       reads=KT2.keys((ti * 128, (ti + 1) * 128)), writes=pk,
                       psize=64, pbase=base, tag="l1a_kt")
                p.emit("vector", lambda e,
                       o=KTOK.ap((half * 512, (half + 1) * 512)),
                       i=pst[:, 0:512]:
                       e.tensor_copy(o, i),
                       reads=pk, writes=KTOK.keys((half * 512, (half + 1) * 512)),
                       tag="l1a_kev")
            if h == 0:
                dbg(ctx, "ve0", VE[0].ap(), VE[0].keys(), cast=True)
                dbg(ctx, "ktok", KTOK.ap(), KTOK.keys(), cast=True)
            sp = (base, base + 64)
            p.emit("vector", lambda e, o=SF.ap(parts=sp): e.memset(o, 0.0),
                   writes=SF.keys(), tag="l1a_s0")
            p.emit("vector", lambda e, o=SB.ap(parts=sp): e.memset(o, 0.0),
                   writes=SB.keys(), tag="l1a_s0b")
            for ci2 in range(NT):
                pcol = (ci2 % 2) * 512
                opk = psk("PSB", pcol, pcol + 64)
                apk = psk("PSC", 0, 128)
                dpk = psk("PSC", 128, 192)
                mm(ctx, psc[:, 0:128],
                   KT2.ap((ci2 * 128, (ci2 + 1) * 128), parts=(base, base + 64)),
                   QT2.ap((ci2 * 128, (ci2 + 1) * 128), parts=(base, base + 64)),
                   reads=KT2.keys((ci2 * 128, (ci2 + 1) * 128)) +
                         QT2.keys((ci2 * 128, (ci2 + 1) * 128)),
                   writes=apk, tag="l1a_amm")
                p.emit("vector", lambda e, o=AMB.ap(), i=psc[:, 0:128],
                       m=cb_ap(ctx, "RETM", (h * 128, (h + 1) * 128)):
                       e.tensor_mul(o, i, m),
                       reads=apk + [CB_KEY], writes=AMB.keys(), tag="l1a_amask")
                mm(ctx, psb[:, pcol:pcol + 64], AMB.ap(),
                   VE[sub].ap((ci2 * 65, ci2 * 65 + 64)),
                   reads=AMB.keys() + VE[sub].keys(), writes=opk,
                   start=True, stop=False, tag="l1a_oi")
                mm(ctx, psb[:, pcol:pcol + 64],
                   QT2.ap((ci2 * 128, (ci2 + 1) * 128), parts=sp),
                   SB.ap(parts=sp),
                   reads=QT2.keys((ci2 * 128, (ci2 + 1) * 128)) + SB.keys(),
                   writes=opk, start=False, stop=True, tag="l1a_oc")
                p.emit("vector", lambda e, o=OH.ap((ci2 * 64, (ci2 + 1) * 64)),
                       i=psb[:, pcol:pcol + 64], s=cf_ap(ctx, "OSC", (h, h + 1)):
                       e.tensor_scalar_mul(o, i, s),
                       reads=opk + [CF_KEY], writes=OH.keys((ci2 * 64, (ci2 + 1) * 64)),
                       tag="l1a_oev")
                p.emit("vector", lambda e, o=KSB.ap(), i=KTOK.ap((ci2 * 64, (ci2 + 1) * 64)),
                       s=cf_ap(ctx, "KSC", (h, h + 1)):
                       e.tensor_scalar_mul(o, i, s),
                       reads=KTOK.keys((ci2 * 64, (ci2 + 1) * 64)) + [CF_KEY],
                       writes=KSB.keys(), tag="l1a_ks")
                mm(ctx, psc[base:base + 64, 128:192], KSB.ap(),
                   VE[sub].ap((ci2 * 65, ci2 * 65 + 64)),
                   reads=KSB.keys() + VE[sub].keys(), writes=dpk,
                   tile_position=(0, base), tag="l1a_ds")
                p.emit("vector", lambda e, o=SF.ap(parts=sp), i=SF.ap(parts=sp),
                       s=g128:
                       e.tensor_scalar_mul(o, i, s),
                       reads=SF.keys(), writes=SF.keys(), tag="l1a_sg")
                p.emit("vector", lambda e, o=SF.ap(parts=sp),
                       i=psc[base:base + 64, 128:192], f=SF.ap(parts=sp), s=g1:
                       e.scalar_tensor_tensor(o, i, s, f, ALU.mult, ALU.add),
                       reads=dpk + SF.keys(), writes=SF.keys(), tag="l1a_su")
                p.emit("vector", lambda e, o=SB.ap(parts=sp), i=SF.ap(parts=sp):
                       e.tensor_copy(o, i),
                       reads=SF.keys(), writes=SB.keys(), tag="l1a_sb")
                if h == 0 and ci2 == 0:
                    dbg(ctx, "amb0", AMB.ap(), AMB.keys(), cast=True)
                    dbg(ctx, "sf0", SF.ap(parts=(0, 64)), SF.keys())
            if h == 0:
                dbg(ctx, "oh0", OH.ap(), OH.keys(), cast=True)
            p.emit("vector", lambda e, o=t["FAR"][:, STATF.lo:STATF.lo + 1], i=OH.ap():
                   e.reduce_sum(o, i, axis=AX.X),
                   reads=OH.keys(), writes=STATF.keys(), tag="l1a_gs")
            p.emit("scalar", lambda e, o=OSQ.ap(), i=OH.ap(),
                   so=t["FAR"][:, STATF.lo + 1:STATF.lo + 2]:
                   e.activation(o, i, AF.Square, accum_out=so),
                   reads=OH.keys(), writes=OSQ.keys() + STATF.keys(), tag="l1a_gsq")
            for j, dst_small in ((0, 8), (1, 9)):
                mm(ctx, psc[0:1, 256 + j:257 + j],
                   t["FAR"][:, STATF.lo + j:STATF.lo + j + 1],
                   cf_ap(ctx, "ONESF"),
                   reads=STATF.keys() + [CF_KEY], writes=psk("PSC", 256, 320),
                   tag="l1a_tot")
            p.emit("vector", lambda e, o=sm[0:1, 8:10], i=psc[0:1, 256:258]:
                   e.tensor_copy(o, i),
                   reads=psk("PSC", 256, 320), writes=[("SMALL", 8)], tag="l1a_tev")
            cnt = 1.0 / (T * HD)
            p.emit("vector", lambda e, o=sm[0:1, 10:11], i=sm[0:1, 8:9]:
                   e.tensor_scalar_mul(o, i, cnt), reads=[("SMALL", 8)],
                   writes=[("SMALL", 10)], tag="l1a_mean")
            p.emit("vector", lambda e, o=sm[0:1, 11:12], m=sm[0:1, 10:11]:
                   e.tensor_mul(o, m, m), reads=[("SMALL", 10)], writes=[("SMALL", 11)],
                   tag="l1a_m2")
            p.emit("vector", lambda e, o=sm[0:1, 12:13], i=sm[0:1, 9:10],
                   m2=sm[0:1, 11:12]:
                   e.scalar_tensor_tensor(o, i, cnt, m2, ALU.mult, ALU.subtract),
                   reads=[("SMALL", 8), ("SMALL", 11)], writes=[("SMALL", 12)],
                   tag="l1a_var")
            p.emit("scalar", lambda e, o=sm[0:1, 13:14], i=sm[0:1, 12:13]:
                   e.activation(o, i, AF.Ln, bias=float(GN1_EPS)),
                   reads=[("SMALL", 12)], writes=[("SMALL", 13)], tag="l1a_ln")
            p.emit("scalar", lambda e, o=sm[0:1, 14:15], i=sm[0:1, 13:14]:
                   e.activation(o, i, AF.Exp, scale=-0.5),
                   reads=[("SMALL", 13)], writes=[("SMALL", 14)], tag="l1a_rstd")
            p.emit("vector", lambda e, o=sm[0:1, 16 + h:17 + h], i=sm[0:1, 10:11]:
                   e.tensor_copy(o, i), reads=[("SMALL", 10)],
                   writes=[("SMALL", 16 + h)], tag="l1a_mrow")
            p.emit("vector", lambda e, o=sm[0:1, 32 + h:33 + h], i=sm[0:1, 14:15]:
                   e.tensor_copy(o, i), reads=[("SMALL", 14)],
                   writes=[("SMALL", 32 + h)], tag="l1a_rrow")
            cih = h // 2
            hb = (h % 2) * 64
            for half in range(2):
                pk = pst_keys(ctx)
                for o8 in range(8):
                    ti = half * 8 + o8
                    tr(ctx, pst[0:64, o8 * 128:(o8 + 1) * 128],
                       OH.ap((ti * 64, (ti + 1) * 64)),
                       reads=OH.keys((ti * 64, (ti + 1) * 64)), writes=pk, tag="l1a_ot")
                dst_lo = cih * T + half * 1024
                p.emit("vector", lambda e,
                       o=YNT.ap((dst_lo, dst_lo + 1024), parts=(hb, hb + 64)),
                       i=pst[0:64, 0:1024]:
                       e.tensor_copy(o, i),
                       reads=pk, writes=YNT.keys((dst_lo, dst_lo + 1024)), tag="l1a_oev2")

    dbg(ctx, "small", t["SMALL"][:, 0:64], [("SMALL", i) for i in range(64)])
    dbg(ctx, "ynt0", YNT.ap((0, 2048)), YNT.keys((0, 2048)), cast=True)
    p.emit("sync", lambda g: g.dma_start(t["FAR"][0:16, MCOL.lo:MCOL.lo + 1],
                                         sm[0:1, 16:32]),
           reads=[("SMALL", 16 + i) for i in range(16)], writes=MCOL.keys(), dma=True,
           tag="l1a_mdma")
    p.emit("sync", lambda g: g.dma_start(t["FAR"][0:16, RCOL.lo:RCOL.lo + 1],
                                         sm[0:1, 32:48]),
           reads=[("SMALL", 32 + i) for i in range(16)], writes=RCOL.keys(), dma=True,
           tag="l1a_rdma")
    pkk = psk("PSC", 0, 512)
    for colbuf, dstbuf in ((MCOL, MBCOL), (RCOL, RBCOL)):
        ofs = 0 if dstbuf is MBCOL else 8
        for a in range(NCH):
            mm(ctx, psc[:, ofs + a:ofs + a + 1],
               cf_ap(ctx, "E16F", (a * 128, (a + 1) * 128), parts=(0, 16)),
               t["FAR"][0:16, colbuf.lo:colbuf.lo + 1],
               reads=colbuf.keys() + [CF_KEY], writes=pkk, tag="l1a_bc")
        p.emit("vector", lambda e, o=t["FAR"][:, dstbuf.lo:dstbuf.lo + 8],
               i=psc[:, ofs:ofs + 8]:
               e.tensor_copy(o, i), reads=pkk, writes=dstbuf.keys(), tag="l1a_bcev")

    def vec_strided(name):
        return bass.AP(t["VEC"], VI[name], [[NCH * NV, 128], [NV, NCH], [1, 1]])

    p.emit("vector", lambda e, o=t["FAR"][:, ALPHACOL.lo:ALPHACOL.lo + 8],
           a=t["FAR"][:, RBCOL.lo:RBCOL.lo + 8], b=vec_strided("L1_attn_gn_weight"):
           e.tensor_mul(o, a, b), reads=RBCOL.keys() + [VEC_KEY],
           writes=ALPHACOL.keys(), tag="l1a_al")
    p.emit("vector", lambda e, o=t["FAR"][:, BCOLF.lo:BCOLF.lo + 8],
           a=t["FAR"][:, MBCOL.lo:MBCOL.lo + 8],
           b=t["FAR"][:, ALPHACOL.lo:ALPHACOL.lo + 8]:
           e.tensor_mul(o, a, b), reads=MBCOL.keys() + ALPHACOL.keys(),
           writes=BCOLF.keys(), tag="l1a_bt0")
    p.emit("vector", lambda e, o=t["FAR"][:, BCOLF.lo:BCOLF.lo + 8],
           a=vec_strided("L1_attn_gn_bias"), b=t["FAR"][:, BCOLF.lo:BCOLF.lo + 8]:
           e.tensor_sub(o, a, b), reads=[VEC_KEY] + BCOLF.keys(),
           writes=BCOLF.keys(), tag="l1a_bt")
    p.emit("vector", lambda e, o=BCOL.ap(), i=t["FAR"][:, BCOLF.lo:BCOLF.lo + 8]:
           e.tensor_copy(o, i), reads=BCOLF.keys(), writes=BCOL.keys(),
           tag="l1a_bcast16")

    WOT = ctx.bar(0, 8192)
    WNAT = ctx.bar(8192, 16384)
    src = t["L1_attn_W_o"][:, :].rearrange("(a p) c -> p a c", p=128)
    p.emit("gpsimd", lambda g, o=WNAT.ap(rearr="p (a c) -> p a c", c=C), i=src:
           g.dma_start(o, i), writes=WNAT.keys(), dma=True, tag="l1a_wold")
    for cj in range(NCH):
        pk = pst_keys(ctx)
        for o8 in range(NCH):
            in_lo = o8 * C + cj * 128
            tr(ctx, pst[:, o8 * 128:(o8 + 1) * 128], WNAT.ap((in_lo, in_lo + 128)),
               reads=WNAT.keys((in_lo, in_lo + 128)), writes=pk, tag="l1a_wot")
        dst_lo = cj * C
        p.emit("vector", lambda e, o=WOT.ap((dst_lo, dst_lo + 1024)), i=pst[:, 0:1024]:
               e.tensor_copy(o, i),
               reads=pk, writes=WOT.keys((dst_lo, dst_lo + 1024)),
               tag="l1a_woev")
    for half in range(2):
        pkk = psk("PSC", 0, 512)
        for a in range(NCH):
            mm(ctx, psc[0:1, 0:512], BCOL.ap((a, a + 1)),
               WOT.ap((a * C + half * 512, a * C + (half + 1) * 512)),
               reads=BCOL.keys() + WOT.keys((a * C + half * 512, a * C + (half + 1) * 512)),
               writes=pkk, start=(a == 0), stop=(a == NCH - 1), tag="l1a_brow")
        p.emit("vector", lambda e, o=BROWB.ap((half * 512, (half + 1) * 512), parts=(0, 1)),
               i=psc[0:1, 0:512]:
               e.tensor_copy(o, i), reads=pkk, writes=BROWB.keys(), tag="l1a_brev")
    for cj in range(NCH):
        lo = cj * C
        p.emit("vector", lambda e, o=WOT.ap((lo, lo + 1024)), i=WOT.ap((lo, lo + 1024)),
               s=t["FAR"][:, ALPHACOL.lo + cj:ALPHACOL.lo + cj + 1]:
               e.tensor_scalar_mul(o, i, s),
               reads=WOT.keys((lo, lo + 1024)) + ALPHACOL.keys(),
               writes=WOT.keys((lo, lo + 1024)), tag="l1a_wsc")

    dbg(ctx, "acol", t["FAR"][:, ALPHACOL.lo:ALPHACOL.lo + 8], ALPHACOL.keys())
    dbg(ctx, "bcolf", t["FAR"][:, BCOLF.lo:BCOLF.lo + 8], BCOLF.keys())
    dbg(ctx, "wot0", WOT.ap((0, 2048)), WOT.keys((0, 2048)), cast=True)
    SQEP = OSQ
    for ti in range(NT):
        pcol = 0
        pbk = psk("PSB", 0, 1024)
        for half in range(2):
            for a in range(NCH):
                mm(ctx, psb[:, half * 512:(half + 1) * 512],
                   YNT.ap((a * T + ti * 128, a * T + (ti + 1) * 128)),
                   WOT.ap((a * C + half * 512, a * C + (half + 1) * 512)),
                   reads=YNT.keys((a * T + ti * 128, a * T + (ti + 1) * 128)) +
                         WOT.keys((a * C + half * 512, a * C + (half + 1) * 512)),
                   writes=pbk, start=(a == 0), stop=False, tag="l1a_out")
            mm(ctx, psb[:, half * 512:(half + 1) * 512],
               cb_ap(ctx, "ONES", (0, 128), parts=(0, 1)),
               BROWB.ap((half * 512, (half + 1) * 512), parts=(0, 1)),
               reads=[CB_KEY] + BROWB.keys(), writes=pbk, start=False, stop=True,
               tag="l1a_badd")
        stage_epilogue_tile(ctx, ti, t["PSB"], 0, pbk, xm, ZF, SQEP)






def build_rows_bf16(ctx, names, rowbuf):
    p, t = ctx.p, ctx.t
    psc = t["PSC"]
    for ri, name in enumerate(names):
        for grp in range(2):
            pk = psk("PSC", 0, 512)
            for a4 in range(4):
                a = grp * 4 + a4
                tr(ctx, psc[0:1, a4 * 128:(a4 + 1) * 128],
                   vec_col(ctx, name, a), reads=[VEC_KEY], writes=pk, f32=True,
                   tag="rowtr")
            lo = ri * C + grp * 512
            p.emit("vector", lambda e, o=rowbuf.ap((lo, lo + 512), parts=(0, 1)),
                   i=psc[0:1, 0:512]:
                   e.tensor_copy(o, i), reads=pk,
                   writes=rowbuf.keys((lo, lo + 512)), tag="rowev")


L0F_UPTO = "all"
L0F_ATTN = "all"


def stage_l0f(ctx, xm):
    upto = L0F_UPTO
    p, nc, t = ctx.p, ctx.nc, ctx.t
    psa, psb, psc, pst = t["PSA"], t["PSB"], t["PSC"], t["PST"]
    sm = t["SMALL"]

    ctx.dram("L0_ffn_Wq", (C, C))
    ctx.dram("L0_ffn_Wk", (C, C))
    ctx.dram("L0_ffn_Wv", (C, C))
    ctx.dram("L0_ffn_Wr", (C, 16))
    ctx.dram("L0_ffn_Wo", (256, C))

    XT = ctx.bar(0, 16384)
    XBS = ctx.bar(16384, 32768)
    QFULL = ctx.bar(16384, 32768)
    OSEL = ctx.bar(16384, 20480)
    OSELT = ctx.bar(20480, 24576)
    PTS = ctx.bar(24576, 25088)
    PTSB = ctx.bar(25088, 25600)
    WA = ctx.bar(32768, 40960)
    QSEL = ctx.bar(40960, 45056)
    KSEL = ctx.bar(45056, 49152)
    VE = ctx.bar(49152, 53376)
    GBUF = ctx.bar(53376, 55424)
    ROWS = ctx.bar(55424, 59520)
    OH4 = ctx.bar(59520, 59536)
    GV4 = ctx.bar(59536, 59537)
    OEXPB = ctx.bar(59538, 59794)
    ZF = ctx.far(16384, 17408)
    MEANX = ctx.far(17472, 17480)
    WRF = ctx.far(17480, 17608)
    GVEC = ctx.far(17608, 17610)
    SQEP = ctx.bar(53376, 54400)

    build_xt(ctx, xm, XT, XBS)

    build_rows_bf16(ctx, ["L0_ffn_bq", "L0_ffn_bk", "L0_ffn_bv", "L0_ffn_bo"], ROWS)

    p.emit("sync", lambda g: g.dma_start(
        t["FAR"][:, WRF.lo:WRF.lo + 128].rearrange("p (a c) -> p a c", c=16),
        t["L0_ffn_Wr"][:, :].rearrange("(a p) c -> p a c", p=128)),
        writes=WRF.keys(), dma=True, tag="l0f_wr")
    for ci in range(NCH):
        p.emit("vector", lambda e, o=t["FAR"][:, MEANX.lo + ci:MEANX.lo + ci + 1],
               i=XT.ap((ci * T, (ci + 1) * T)):
               e.reduce_sum(o, i, axis=AX.X),
               reads=XT.keys((ci * T, (ci + 1) * T)), writes=MEANX.keys(),
               tag="l0f_mx")
    p.emit("vector", lambda e, o=t["FAR"][:, MEANX.lo:MEANX.lo + 8],
           i=t["FAR"][:, MEANX.lo:MEANX.lo + 8]:
           e.tensor_scalar_mul(o, i, 1.0 / T),
           reads=MEANX.keys(), writes=MEANX.keys(), tag="l0f_mxs")
    pkk = psk("PSC", 0, 512)
    for ci in range(NCH):
        mm(ctx, psc[0:16, 0:1], t["FAR"][:, WRF.lo + ci * 16:WRF.lo + (ci + 1) * 16],
           t["FAR"][:, MEANX.lo + ci:MEANX.lo + ci + 1],
           reads=WRF.keys() + MEANX.keys(), writes=pkk,
           start=(ci == 0), stop=(ci == NCH - 1), tag="l0f_dmm")
    p.emit("vector", lambda e, o=sm[0:16, 100:101], i=psc[0:16, 0:1],
           b=vec_col(ctx, "L0_ffn_br", 0, parts=(0, 16)):
           e.tensor_add(o, i, b),
           reads=pkk + [VEC_KEY], writes=[("SMALL", 100)], tag="l0f_dbr")
    tr(ctx, psc[0:1, 0:16], sm[0:16, 100:101], reads=[("SMALL", 100)],
       writes=pkk, f32=True, psize=16, tag="l0f_dtr")
    p.emit("vector", lambda e, o=sm[0:1, 101:117], i=psc[0:1, 0:16]:
           e.tensor_copy(o, i), reads=pkk, writes=[("SMALL", 101 + i) for i in range(16)],
           tag="l0f_drow")
    SK = lambda c: ("SMALL", c)
    ROW = lambda a, b: sm[0:1, a:b]
    p.emit("vector", lambda e, o=ROW(117, 118), i=ROW(101, 117):
           e.reduce_max(o, i, axis=AX.X, negate=True),
           reads=[SK(101 + i) for i in range(16)], writes=[SK(117)], tag="l0f_dm")
    p.emit("scalar", lambda e, o=ROW(118, 134), i=ROW(101, 117), b=ROW(117, 118):
           e.activation(o, i, AF.Exp, bias=b),
           reads=[SK(101 + i) for i in range(16)] + [SK(117)],
           writes=[SK(118 + i) for i in range(16)], tag="l0f_de")
    p.emit("vector", lambda e, o=ROW(134, 135), i=ROW(118, 134):
           e.reduce_sum(o, i, axis=AX.X),
           reads=[SK(118 + i) for i in range(16)], writes=[SK(134)], tag="l0f_ds")
    p.emit("vector", lambda e, o=ROW(135, 136), i=ROW(134, 135):
           e.reciprocal(o, i), reads=[SK(134)], writes=[SK(135)], tag="l0f_dr")
    p.emit("vector", lambda e, o=ROW(136, 152), i=ROW(118, 134), s=ROW(135, 136):
           e.tensor_scalar_mul(o, i, s),
           reads=[SK(118 + i) for i in range(16)] + [SK(135)],
           writes=[SK(136 + i) for i in range(16)], tag="l0f_dd")
    p.emit("vector", lambda e, o=ROW(200, 216), i=ROW(136, 152):
           e.tensor_copy(o, i),
           reads=[SK(136 + i) for i in range(16)],
           writes=[SK(200 + i) for i in range(16)], tag="l0f_t0")
    for kk4 in range(4):
        mcol = 216 + kk4
        vcol = 380 + kk4
        ohlo = 220 + kk4 * 16
        dlo = 400
        p.emit("vector", lambda e, o=ROW(mcol, mcol + 1), i=ROW(200, 216):
               e.reduce_max(o, i, axis=AX.X, negate=True),
               reads=[SK(200 + i) for i in range(16)], writes=[SK(mcol)],
               tag="l0f_tm")
        p.emit("scalar", lambda e, o=ROW(vcol, vcol + 1), i=ROW(mcol, mcol + 1):
               e.activation(o, i, AF.Identity, scale=-1.0),
               reads=[SK(mcol)], writes=[SK(vcol)], tag="l0f_tv")
        p.emit("scalar", lambda e, o=ROW(dlo, dlo + 16), i=ROW(200, 216),
               b=ROW(mcol, mcol + 1):
               e.activation(o, i, AF.Identity, bias=b),
               reads=[SK(200 + i) for i in range(16)] + [SK(mcol)],
               writes=[SK(dlo + i) for i in range(16)], tag="l0f_td")
        p.emit("scalar", lambda e, o=ROW(dlo + 16, dlo + 32), i=ROW(dlo, dlo + 16):
               e.activation(o, i, AF.Sign, bias=1e-9),
               reads=[SK(dlo + i) for i in range(16)],
               writes=[SK(dlo + 16 + i) for i in range(16)], tag="l0f_tsg")
        p.emit("scalar", lambda e, o=ROW(ohlo, ohlo + 16), i=ROW(dlo + 16, dlo + 32):
               e.activation(o, i, AF.Identity, scale=0.5, bias=0.5),
               reads=[SK(dlo + 16 + i) for i in range(16)],
               writes=[SK(ohlo + i) for i in range(16)], tag="l0f_toh")
        p.emit("vector", lambda e, o=ROW(200, 216), i=ROW(ohlo, ohlo + 16),
               d=ROW(200, 216):
               e.scalar_tensor_tensor(o, i, -1e30, d, ALU.mult, ALU.add),
               reads=[SK(ohlo + i) for i in range(16)] + [SK(200 + i) for i in range(16)],
               writes=[SK(200 + i) for i in range(16)], tag="l0f_tmask")
    p.emit("vector", lambda e, o=ROW(290, 306): e.memset(o, 0.0),
           writes=[SK(290 + i) for i in range(16)], tag="l0f_sp0")
    for kk4 in range(4):
        ohlo = 220 + kk4 * 16
        vcol = 380 + kk4
        p.emit("vector", lambda e, o=ROW(290, 306), i=ROW(ohlo, ohlo + 16),
               s=ROW(vcol, vcol + 1), d=ROW(290, 306):
               e.scalar_tensor_tensor(o, i, s, d, ALU.mult, ALU.add),
               reads=[SK(ohlo + i) for i in range(16)] + [SK(vcol)] +
                     [SK(290 + i) for i in range(16)],
               writes=[SK(290 + i) for i in range(16)], tag="l0f_spa")
    p.emit("scalar", lambda e, o=ROW(306, 322), i=ROW(290, 306):
           e.activation(o, i, AF.Exp),
           reads=[SK(290 + i) for i in range(16)],
           writes=[SK(306 + i) for i in range(16)], tag="l0f_ga")
    p.emit("vector", lambda e, o=ROW(322, 323), i=ROW(306, 322):
           e.reduce_sum(o, i, axis=AX.X),
           reads=[SK(306 + i) for i in range(16)], writes=[SK(322)], tag="l0f_gs")
    p.emit("vector", lambda e, o=ROW(323, 324), i=ROW(322, 323):
           e.reciprocal(o, i), reads=[SK(322)], writes=[SK(323)], tag="l0f_gr")
    for kk4 in range(4):
        ohlo = 220 + kk4 * 16
        p.emit("vector", lambda e, o=ROW(324, 340) if False else ROW(340, 356),
               i=ROW(306, 322), h=ROW(ohlo, ohlo + 16):
               e.tensor_mul(o, i, h),
               reads=[SK(306 + i) for i in range(16)] + [SK(ohlo + i) for i in range(16)],
               writes=[SK(340 + i) for i in range(16)], tag="l0f_gm")
        p.emit("vector", lambda e, o=ROW(356 + kk4, 357 + kk4), i=ROW(340, 356):
               e.reduce_sum(o, i, axis=AX.X),
               reads=[SK(340 + i) for i in range(16)], writes=[SK(356 + kk4)],
               tag="l0f_gsum")
        p.emit("vector", lambda e, o=ROW(330 + kk4, 331 + kk4),
               i=ROW(356 + kk4, 357 + kk4), s=ROW(323, 324):
               e.tensor_scalar_mul(o, i, s),
               reads=[SK(356 + kk4), SK(323)], writes=[SK(330 + kk4)], tag="l0f_gk")
    p.emit("gpsimd", lambda g, o=bass.AP(t["BAR"], OH4.lo, [[BAR_COLS, 4], [1, 16]]),
           i=bass.AP(t["FAR"] if False else t["SMALL"], 220, [[512, 1], [16, 4], [1, 16]]):
           g.dma_start(o, i),
           reads=[SK(220 + i) for i in range(64)], writes=OH4.keys(), dma=True,
           tag="l0f_ohdma")
    p.emit("gpsimd", lambda g, o=bass.AP(t["BAR"], GV4.lo, [[BAR_COLS, 4], [1, 1]]),
           i=bass.AP(t["SMALL"], 330, [[512, 1], [1, 4], [1, 1]]):
           g.dma_start(o, i),
           reads=[SK(330 + i) for i in range(4)], writes=GV4.keys(), dma=True,
           tag="l0f_gvdma")
    pko = psk("PSC", 0, 512)
    mm(ctx, psc[0:16, 256:512], bass.AP(t["BAR"], OH4.lo, [[BAR_COLS, 4], [1, 16]]),
       cb_ap(ctx, "SEL4", parts=(0, 4)),
       reads=OH4.keys() + [CB_KEY], writes=pko, tag="l0f_oexp")
    p.emit("vector", lambda e, o=OEXPB.ap(parts=(0, 16)), i=psc[0:16, 256:512]:
           e.tensor_copy(o, i), reads=pko, writes=OEXPB.keys(), tag="l0f_oee")
    for ci in range(NCH):
        pkg = psk("PSB", 0, 512)
        mm(ctx, psb[:, 0:256], cb_ap(ctx, "E16", (ci * 128, (ci + 1) * 128), parts=(0, 16)),
           OEXPB.ap(parts=(0, 16)),
           reads=[CB_KEY] + OEXPB.keys(), writes=pkg, tag="l0f_bigo")
        p.emit("vector", lambda e, o=GBUF.ap((ci * 256, (ci + 1) * 256)),
               i=psb[:, 0:256], r=cb_ap(ctx, "REP"):
               e.tensor_mul(o, i, r),
               reads=pkg + [CB_KEY], writes=GBUF.keys((ci * 256, (ci + 1) * 256)),
               tag="l0f_g")
    for m2 in range(2):
        pkg = psk("PSB", 512, 1024)
        mm(ctx, psb[:, 512 + m2:513 + m2],
           cb_ap(ctx, "SEL4", (m2 * 128, (m2 + 1) * 128), parts=(0, 4)),
           bass.AP(t["BAR"], GV4.lo, [[BAR_COLS, 4], [1, 1]]),
           reads=[CB_KEY] + GV4.keys(), writes=pkg, tag="l0f_gv")
    p.emit("vector", lambda e, o=t["FAR"][:, GVEC.lo:GVEC.lo + 2], i=psb[:, 512:514]:
           e.tensor_copy(o, i), reads=psk("PSB", 512, 1024), writes=GVEC.keys(),
           tag="l0f_gve")

    dbg(ctx, "l0fsm", t["SMALL"][:, 0:512], [("SMALL", i) for i in range(512)])
    dbg(ctx, "l0fg", GBUF.ap(), GBUF.keys(), cast=True)
    if upto == "gate":
        return
    for wname, brow, dstb, trans_v in (
            ("L0_ffn_Wq", 0, QSEL, False), ("L0_ffn_Wk", 1, KSEL, False),
            ("L0_ffn_Wv", 2, None, True)):
        src = t[wname][:, :].rearrange("(a p) c -> p a c", p=128)
        p.emit("gpsimd", lambda g, o=WA.ap(rearr="p (a c) -> p a c", c=C), i=src:
               g.dma_start(o, i), writes=WA.keys(), dma=True, tag="l0f_wld")
        for mi in range(NCH):
            for nchk in range(4):
                pkq = psk("PSA", nchk * 512, (nchk + 1) * 512)
                for a in range(NCH):
                    mm(ctx, psa[:, nchk * 512:(nchk + 1) * 512],
                       WA.ap((a * C + mi * 128, a * C + (mi + 1) * 128)),
                       XT.ap((a * T + nchk * 512, a * T + (nchk + 1) * 512)),
                       reads=WA.keys((a * C + mi * 128, a * C + (mi + 1) * 128)) +
                             XT.keys((a * T + nchk * 512, a * T + (nchk + 1) * 512)),
                       writes=pkq, start=(a == 0), stop=False, tag="l0f_proj")
                mm(ctx, psa[:, nchk * 512:(nchk + 1) * 512],
                   ROWS.ap((brow * C + mi * 128, brow * C + (mi + 1) * 128), parts=(0, 1)),
                   cb_ap(ctx, "ONES", (0, 512), parts=(0, 1)),
                   reads=ROWS.keys((brow * C + mi * 128, brow * C + (mi + 1) * 128)) + [CB_KEY],
                   writes=pkq, start=False, stop=True, tag="l0f_bias")
                p.emit("vector", lambda e,
                       o=QFULL.ap((mi * T + nchk * 512, mi * T + (nchk + 1) * 512)),
                       i=psa[:, nchk * 512:(nchk + 1) * 512]:
                       e.tensor_copy(o, i),
                       reads=pkq,
                       writes=QFULL.keys((mi * T + nchk * 512, mi * T + (nchk + 1) * 512)),
                       tag="l0f_pev")
        for m2 in range(2):
            for nchk in range(4):
                pkq = psk("PSA", nchk * 512, (nchk + 1) * 512)
                for mi in range(NCH):
                    mm(ctx, psa[:, nchk * 512:(nchk + 1) * 512],
                       GBUF.ap((mi * 256 + m2 * 128, mi * 256 + (m2 + 1) * 128)),
                       QFULL.ap((mi * T + nchk * 512, mi * T + (nchk + 1) * 512)),
                       reads=GBUF.keys((mi * 256 + m2 * 128, mi * 256 + (m2 + 1) * 128)) +
                             QFULL.keys((mi * T + nchk * 512, mi * T + (nchk + 1) * 512)),
                       writes=pkq, start=(mi == 0), stop=(mi == NCH - 1), tag="l0f_gat")
                if not trans_v:
                    p.emit("vector", lambda e,
                           o=dstb.ap((m2 * T + nchk * 512, m2 * T + (nchk + 1) * 512)),
                           i=psa[:, nchk * 512:(nchk + 1) * 512],
                           s=t["FAR"][:, GVEC.lo + m2:GVEC.lo + m2 + 1]:
                           e.tensor_scalar_mul(o, i, s),
                           reads=pkq + GVEC.keys(),
                           writes=dstb.keys((m2 * T + nchk * 512, m2 * T + (nchk + 1) * 512)),
                           tag="l0f_gev")
                else:
                    p.emit("vector", lambda e,
                           o=QFULL.ap((m2 * T + nchk * 512, m2 * T + (nchk + 1) * 512)),
                           i=psa[:, nchk * 512:(nchk + 1) * 512],
                           s=t["FAR"][:, GVEC.lo + m2:GVEC.lo + m2 + 1]:
                           e.tensor_scalar_mul(o, i, s),
                           reads=pkq + GVEC.keys(),
                           writes=QFULL.keys((m2 * T + nchk * 512, m2 * T + (nchk + 1) * 512)),
                           tag="l0f_vev")
    p.emit("vector", lambda e, o=bass.AP(t["BAR"], VE.lo + 64,
                                         [[BAR_COLS, 128], [66, 64], [1, 2]]):
           e.memset(o, 1.0), writes=VE.keys(), tag="l0f_ones")
    for r4 in range(4):
        tile = r4 // 2
        base = (r4 % 2) * 64
        for half in range(2):
            pk = pst_keys(ctx)
            for o8 in range(8):
                ti = half * 8 + o8
                tr(ctx, pst[:, o8 * 64:(o8 + 1) * 64],
                   QFULL.ap((tile * T + ti * 128, tile * T + (ti + 1) * 128),
                            parts=(base, base + 64)),
                   reads=QFULL.keys((tile * T + ti * 128, tile * T + (ti + 1) * 128)),
                   writes=pk, psize=64, pbase=base, tag="l0f_vt")
            p.emit("vector", lambda e,
                   o=bass.AP(t["BAR"], VE.lo + r4 * 1056 + half * 8 * 66,
                             [[BAR_COLS, 128], [66, 8], [1, 64]]),
                   i=pst[:, 0:512]:
                   e.tensor_copy(o, i),
                   reads=pk, writes=VE.keys(), tag="l0f_vev2")

    dbg(ctx, "l0fqs", QSEL.ap((0, 2048)), QSEL.keys((0, 2048)), cast=True)
    dbg(ctx, "l0fve", VE.ap((0, 2048)), VE.keys((0, 2048)), cast=True)
    if upto == "proj":
        return
    for r4 in range(4):
        tile = r4 // 2
        base = (r4 % 2) * 64
        for quarter in range(4):
            pob = psk("PSB", (quarter % 2) * 512, (quarter % 2) * 512 + 264)

            def emit_smm(tk, tile=tile, base=base, quarter=quarter):
                pks = psk("PSA", (tk % 4) * 512, ((tk % 4) + 1) * 512)
                mm(ctx, psa[:, (tk % 4) * 512:((tk % 4) + 1) * 512],
                   KSEL.ap((tile * T + tk * 128, tile * T + (tk + 1) * 128),
                           parts=(base, base + 64)),
                   QSEL.ap((tile * T + quarter * 512, tile * T + (quarter + 1) * 512),
                           parts=(base, base + 64)),
                   reads=KSEL.keys((tile * T + tk * 128, tile * T + (tk + 1) * 128)) +
                         QSEL.keys((tile * T + quarter * 512, tile * T + (quarter + 1) * 512)),
                   writes=pks, tag="l0f_smm")
                return pks

            pks_q = {0: emit_smm(0)}
            for tk in range(16):
                if tk + 1 <= 15:
                    pks_q[tk + 1] = emit_smm(tk + 1)
                pks = pks_q.pop(tk)
                PTSP = PTS if tk % 2 == 0 else PTSB
                p.emit("scalar", lambda e, o=PTSP.ap(),
                       i=psa[:, (tk % 4) * 512:((tk % 4) + 1) * 512]:
                       e.activation(o, i, AF.Exp),
                       reads=pks, writes=PTSP.keys(), tag="l0f_exp")
                if L0F_ATTN == "sx":
                    continue
                for sub in range(4):
                    mm(ctx, psb[:, (quarter % 2) * 512 + sub * 66:
                                (quarter % 2) * 512 + sub * 66 + 66],
                       PTSP.ap((sub * 128, (sub + 1) * 128)),
                       bass.AP(t["BAR"], VE.lo + r4 * 1056 + tk * 66,
                               [[BAR_COLS, 128], [1, 66]]),
                       reads=PTSP.keys() + VE.keys(),
                       writes=pob, start=(tk == 0 and sub == 0),
                       stop=(tk == 15 and sub == 3), tag="l0f_av")
            if L0F_ATTN == "sav":
                continue
            for sub in range(4):
                rcol = 360 + sub
                p.emit("vector", lambda e, o=sm[:, rcol + 8:rcol + 9],
                       i=psb[:, (quarter % 2) * 512 + sub * 66 + 64:
                             (quarter % 2) * 512 + sub * 66 + 65]:
                       e.tensor_copy(o, i), reads=pob, writes=[SK(rcol + 8)],
                       tag="l0f_rcp")
                p.emit("vector", lambda e, o=sm[:, rcol:rcol + 1],
                       i=sm[:, rcol + 8:rcol + 9]:
                       e.reciprocal(o, i), reads=[SK(rcol + 8)], writes=[SK(rcol)],
                       tag="l0f_rc")
                ti = quarter * 4 + sub
                p.emit("scalar", lambda e,
                       o=OSEL.ap((ti * 256 + r4 * 64, ti * 256 + (r4 + 1) * 64)),
                       i=psb[:, (quarter % 2) * 512 + sub * 66:
                             (quarter % 2) * 512 + sub * 66 + 64],
                       s=sm[:, rcol:rcol + 1]:
                       e.activation(o, i, AF.Identity, scale=s),
                       reads=pob + [SK(rcol)],
                       writes=OSEL.keys((ti * 256 + r4 * 64, ti * 256 + (r4 + 1) * 64)),
                       tag="l0f_oev")

    if upto == "attn":
        return
    for kt in range(2):
        for half in range(2):
            pk = pst_keys(ctx)
            for o8 in range(8):
                ti = half * 8 + o8
                tr(ctx, pst[:, o8 * 128:(o8 + 1) * 128],
                   OSEL.ap((ti * 256 + kt * 128, ti * 256 + (kt + 1) * 128)),
                   reads=OSEL.keys((ti * 256 + kt * 128, ti * 256 + (kt + 1) * 128)),
                   writes=pk, tag="l0f_ot")
            dst_lo = kt * T + half * 1024
            p.emit("vector", lambda e, o=OSELT.ap((dst_lo, dst_lo + 1024)),
                   i=pst[:, 0:1024]:
                   e.tensor_copy(o, i),
                   reads=pk, writes=OSELT.keys((dst_lo, dst_lo + 1024)), tag="l0f_ote")

    WOB = ctx.bar(40960, 43008)
    p.emit("gpsimd", lambda g, o=WOB.ap(rearr="p (a c) -> p a c", c=C),
           i=t["L0_ffn_Wo"][:, :].rearrange("(a p) c -> p a c", p=128):
           g.dma_start(o, i), writes=WOB.keys(), dma=True, tag="l0f_wo")
    for ti in range(NT):
        pbk = psk("PSB", 0, 1024)
        for half in range(2):
            for kt in range(2):
                mm(ctx, psb[:, half * 512:(half + 1) * 512],
                   OSELT.ap((kt * T + ti * 128, kt * T + (ti + 1) * 128)),
                   WOB.ap((kt * C + half * 512, kt * C + (half + 1) * 512)),
                   reads=OSELT.keys((kt * T + ti * 128, kt * T + (ti + 1) * 128)) +
                         WOB.keys((kt * C + half * 512, kt * C + (half + 1) * 512)),
                   writes=pbk, start=(kt == 0), stop=False, tag="l0f_out")
            mm(ctx, psb[:, half * 512:(half + 1) * 512],
               cb_ap(ctx, "ONES", (0, 128), parts=(0, 1)),
               ROWS.ap((3 * C + half * 512, 3 * C + (half + 1) * 512), parts=(0, 1)),
               reads=[CB_KEY] + ROWS.keys((3 * C + half * 512, 3 * C + (half + 1) * 512)),
               writes=pbk, start=False, stop=True, tag="l0f_bo")
        stage_epilogue_tile(ctx, ti, t["PSB"], 0, pbk, xm, ZF, SQEP)





def stage_l1f(ctx, xm):
    p, nc, t = ctx.p, ctx.nc, ctx.t
    psa, psb, psc, pst = t["PSA"], t["PSB"], t["PSC"], t["PST"]
    sm = t["SMALL"]

    for nm, shape in (("L1_ffn_wq", (C, C)), ("L1_ffn_wk", (C, C)),
                      ("L1_ffn_wv", (C, C)), ("L1_ffn_wo", (C, C)),
                      ("L1_ffn_gate_w", (4, C)),
                      ("L1_ffn_e0_up", (2048, C)), ("L1_ffn_e0_down", (C, 2048)),
                      ("L1_ffn_e1_w1", (2048, C)), ("L1_ffn_e1_w2", (2048, C)),
                      ("L1_ffn_e1_down", (C, 2048)),
                      ("L1_ffn_e2_l1", (2048, C)), ("L1_ffn_e2_l2", (2048, 2048)),
                      ("L1_ffn_e2_l3", (2048, 2048)), ("L1_ffn_e2_l4", (C, 2048)),
                      ("L1_ffn_e3_down", (C, C)), ("L1_ffn_e3_up", (2048, C)),
                      ("L1_ffn_e3_out", (C, 2048))):
        ctx.dram(nm, shape)

    XT = ctx.bar(0, 16384)
    XBS = ctx.bar(16384, 32768)
    AOB = ctx.bar(16384, 32768)
    QP = ctx.bar(32768, 34816)
    KP = ctx.bar(34816, 36864)
    VP = ctx.bar(36864, 38912)
    QTOK = ctx.bar(38912, 39936)
    KTOK = ctx.bar(39936, 40960)
    QPT = ctx.bar(40960, 43008)
    KPT = ctx.bar(43008, 45056)
    VE1 = ctx.bar(45056, 47168)
    PTS1 = ctx.bar(47168, 47680)
    PTS1B = ctx.bar(57472, 57984)
    NATQ = [ctx.bar(47680 + i * 1024, 48704 + i * 1024) for i in range(2)]
    NATK = [ctx.bar(49728 + i * 1024, 50752 + i * 1024) for i in range(2)]
    NATV = [ctx.bar(51776 + i * 1024, 52800 + i * 1024) for i in range(2)]
    NQT = ctx.bar(53824, 54848)
    NKT = ctx.bar(54848, 55872)
    NVT = ctx.bar(55872, 56896)
    QKD = ctx.bar(56896, 56960)
    BROW = ctx.bar(56960, 57984)

    ZF = ctx.far(16384, 17408)
    XTF = ctx.far(16384, 17408)
    RSTD1 = ctx.far(17472, 17488)
    RSTDQ = ctx.far(17488, 17504)
    RSTDK = ctx.far(17504, 17520)
    WTS = ctx.far(17520, 17584)
    GWT = ctx.far(17584, 17616)
    BIASH = ctx.far(17616, 17617)
    NQ2 = ctx.far(17617, 17633)

    for ti in range(NT):
        SQS = ctx.bar(58368, 59392)
        p.emit("scalar", lambda e, o=SQS.ap(), i=xm.ap((ti * C, (ti + 1) * C)),
               so=sm[:, 200 + ti:201 + ti]:
               e.activation(o, i, AF.Square, accum_out=so),
               reads=xm.keys((ti * C, (ti + 1) * C)),
               writes=SQS.keys() + [("SMALL", 200 + ti)],
               tag="l1f_sq")
        p.emit("scalar", lambda e, o=sm[:, 216 + ti:217 + ti], i=sm[:, 200 + ti:201 + ti]:
               e.activation(o, i, AF.Ln, bias=float(RMS_EPS), scale=1.0 / 1024),
               reads=[("SMALL", 200 + ti)], writes=[("SMALL", 216 + ti)], tag="l1f_ln1")
        p.emit("scalar", lambda e, o=t["FAR"][:, RSTD1.lo + ti:RSTD1.lo + ti + 1],
               i=sm[:, 216 + ti:217 + ti]:
               e.activation(o, i, AF.Exp, scale=-0.5),
               reads=[("SMALL", 216 + ti)], writes=RSTD1.keys(), tag="l1f_rs1")

    build_xt(ctx, xm, XT, XBS)

    pkq = psk("PSC", 0, 512)
    tr(ctx, psc[0:1, 0:128], vec_col(ctx, "L1_ffn_qn_w_t", 0), reads=[VEC_KEY],
       writes=pkq, f32=True, tag="l1f_qtr")
    p.emit("vector", lambda e, o=sm[0:1, 232:296], i=psc[0:1, 0:64]:
           e.tensor_copy(o, i), reads=pkq,
           writes=[("SMALL", 232 + i) for i in range(64)], tag="l1f_qrow")
    tr(ctx, psc[0:1, 0:128], vec_col(ctx, "L1_ffn_kn_w_t", 0), reads=[VEC_KEY],
       writes=pkq, f32=True, tag="l1f_ktr")
    p.emit("vector", lambda e, o=sm[0:1, 296:360], i=psc[0:1, 0:64]:
           e.tensor_copy(o, i), reads=pkq,
           writes=[("SMALL", 296 + i) for i in range(64)], tag="l1f_krow")
    p.emit("vector", lambda e, o=sm[0:1, 232:296], a=sm[0:1, 232:296], b=sm[0:1, 296:360]:
           e.tensor_mul(o, a, b),
           reads=[("SMALL", 232 + i) for i in range(64)] + [("SMALL", 296 + i) for i in range(64)],
           writes=[("SMALL", 232 + i) for i in range(64)], tag="l1f_qkm")
    p.emit("vector", lambda e, o=BROW.ap((0, 64), parts=(0, 1)), i=sm[0:1, 232:296]:
           e.tensor_copy(o, i), reads=[("SMALL", 232 + i) for i in range(64)],
           writes=BROW.keys((0, 64)), tag="l1f_qkb")
    mm(ctx, psc[:, 0:64], cb_ap(ctx, "ONES", (0, 128), parts=(0, 1)),
       BROW.ap((0, 64), parts=(0, 1)),
       reads=[CB_KEY] + BROW.keys((0, 64)), writes=pkq, tag="l1f_qkbc")
    p.emit("vector", lambda e, o=QKD.ap(), i=psc[:, 0:64]:
           e.tensor_copy(o, i), reads=pkq, writes=QKD.keys(), tag="l1f_qkd")

    for ph in range(8):
        par = ph % 2
        for wname, natb in (("L1_ffn_wq", NATQ[par]), ("L1_ffn_wk", NATK[par]),
                            ("L1_ffn_wv", NATV[par])):
            srcw = t[wname][ph * 128:(ph + 1) * 128, :]
            p.emit("gpsimd", lambda g, o=natb.ap(), i=srcw:
                   g.dma_start(o, i), writes=natb.keys(), dma=True, tag=f"l1f_wld{ph}")
        for natb, wtb, n1s in ((NATQ[par], NQT, True), (NATK[par], NKT, True),
                               (NATV[par], NVT, True)):
            pk = pst_keys(ctx)
            for a in range(NCH):
                tr(ctx, pst[:, a * 128:(a + 1) * 128],
                   natb.ap((a * 128, (a + 1) * 128)),
                   reads=natb.keys(), writes=pk, tag="l1f_wtr")
            p.emit("vector", lambda e, o=wtb.ap(), i=pst[:, 0:1024],
                   s=bass.AP(t["VEC"], VI["L1_ffn_n1_w"], [[NCH * NV, 128], [NV, 8], [0, 128]]):
                   e.tensor_mul(o, i, s),
                   reads=pk + [VEC_KEY], writes=wtb.keys(), tag="l1f_wte")
        for wtb, dstb in ((NQT, QP), (NKT, KP), (NVT, VP)):
            for nchk in range(4):
                pkk = psk("PSA", nchk * 512, (nchk + 1) * 512)
                for a in range(NCH):
                    mm(ctx, psa[:, nchk * 512:(nchk + 1) * 512],
                       wtb.ap((a * 128, (a + 1) * 128)),
                       XT.ap((a * T + nchk * 512, a * T + (nchk + 1) * 512)),
                       reads=wtb.keys() + XT.keys((a * T + nchk * 512, a * T + (nchk + 1) * 512)),
                       writes=pkk, start=(a == 0), stop=(a == NCH - 1), tag="l1f_proj")
                p.emit("vector", lambda e, o=dstb.ap((nchk * 512, (nchk + 1) * 512)),
                       i=psa[:, nchk * 512:(nchk + 1) * 512]:
                       e.tensor_copy(o, i),
                       reads=pkk, writes=dstb.keys((nchk * 512, (nchk + 1) * 512)),
                       tag="l1f_pev")
        for sub in range(2):
            h = 2 * ph + sub
            base = sub * 64
            sp = (base, base + 64)
            for srcb, dstb in ((QP, QTOK), (KP, KTOK)):
                for half in range(2):
                    pk = pst_keys(ctx)
                    for o8 in range(8):
                        ti = half * 8 + o8
                        tr(ctx, pst[:, o8 * 64:(o8 + 1) * 64],
                           srcb.ap((ti * 128, (ti + 1) * 128), parts=sp),
                           reads=srcb.keys((ti * 128, (ti + 1) * 128)), writes=pk,
                           psize=64, pbase=base, tag="l1f_qt")
                    p.emit("vector", lambda e, o=dstb.ap((half * 512, (half + 1) * 512)),
                           i=pst[:, 0:512]:
                           e.tensor_copy(o, i),
                           reads=pk, writes=dstb.keys((half * 512, (half + 1) * 512)),
                           tag="l1f_qte")
            SQS = ctx.bar(58368, 59392)
            for tokb, rstdb in ((QTOK, RSTDQ), (KTOK, RSTDK)):
                p.emit("scalar", lambda e, o=SQS.ap(), i=tokb.ap():
                       e.activation(o, i, AF.Square),
                       reads=tokb.keys(), writes=SQS.keys(),
                       tag="l1f_qsq")
                p.emit("vector", lambda e,
                       o=t["FAR"][:, rstdb.lo:rstdb.lo + 16],
                       i=SQS.ap(rearr="p (a d) -> p a d", d=64):
                       e.reduce_sum(o, i, axis=AX.X),
                       reads=SQS.keys(), writes=rstdb.keys(),
                       tag="l1f_qss")
                p.emit("scalar", lambda e, o=t["FAR"][:, rstdb.lo:rstdb.lo + 16],
                       i=t["FAR"][:, rstdb.lo:rstdb.lo + 16]:
                       e.activation(o, i, AF.Ln, bias=float(RMS_EPS), scale=1.0 / 64),
                       reads=rstdb.keys(), writes=rstdb.keys(), tag="l1f_qln")
                p.emit("scalar", lambda e, o=t["FAR"][:, rstdb.lo:rstdb.lo + 16],
                       i=t["FAR"][:, rstdb.lo:rstdb.lo + 16]:
                       e.activation(o, i, AF.Exp, scale=-0.5),
                       reads=rstdb.keys(), writes=rstdb.keys(), tag="l1f_qex")
            p.emit("vector", lambda e, o=QTOK.ap(rearr="p (a d) -> p a d", d=64),
                   i=QTOK.ap(rearr="p (a d) -> p a d", d=64),
                   s=bass.AP(t["FAR"], RSTDQ.lo, [[FAR_COLS, 128], [1, 16], [0, 64]]):
                   e.tensor_mul(o, i, s),
                   reads=QTOK.keys() + RSTDQ.keys(),
                   writes=QTOK.keys(), tag="l1f_qap")
            p.emit("vector", lambda e, o=QTOK.ap(rearr="p (a d) -> p a d", d=64),
                   i=QTOK.ap(rearr="p (a d) -> p a d", d=64),
                   q=bass.AP(t["BAR"], QKD.lo, [[BAR_COLS, 128], [0, 16], [1, 64]]):
                   e.tensor_mul(o, i, q),
                   reads=QTOK.keys() + QKD.keys(),
                   writes=QTOK.keys(), tag="l1f_qap2")
            p.emit("vector", lambda e, o=KTOK.ap(rearr="p (a d) -> p a d", d=64),
                   i=KTOK.ap(rearr="p (a d) -> p a d", d=64),
                   s=bass.AP(t["FAR"], RSTDK.lo, [[FAR_COLS, 128], [1, 16], [0, 64]]):
                   e.tensor_mul(o, i, s),
                   reads=KTOK.keys() + RSTDK.keys(),
                   writes=KTOK.keys(), tag="l1f_kap")
            p.emit("scalar", lambda e, o=SQS.ap(), i=QTOK.ap():
                   e.activation(o, i, AF.Square),
                   reads=QTOK.keys(), writes=SQS.keys(), tag="l1f_nq")
            p.emit("vector", lambda e, o=t["FAR"][:, NQ2.lo:NQ2.lo + 16],
                   i=SQS.ap(rearr="p (a d) -> p a d", d=64):
                   e.reduce_sum(o, i, axis=AX.X),
                   reads=SQS.keys(), writes=NQ2.keys(), tag="l1f_nqs")
            p.emit("vector", lambda e, o=sm[:, 360:361], i=t["FAR"][:, NQ2.lo:NQ2.lo + 16]:
                   e.reduce_max(o, i, axis=AX.X),
                   reads=NQ2.keys(), writes=[("SMALL", 360)], tag="l1f_nqm")
            pkc = psk("PSC", 0, 512)
            tr(ctx, psc[0:1, 0:128], sm[:, 360:361], reads=[("SMALL", 360)],
               writes=pkc, f32=True, tag="l1f_nqt")
            p.emit("vector", lambda e, o=sm[0:1, 361:362], i=psc[0:1, 0:128]:
                   e.reduce_max(o, i, axis=AX.X), reads=pkc, writes=[("SMALL", 361)],
                   tag="l1f_nqm2")
            p.emit("scalar", lambda e, o=sm[0:1, 362:363], i=sm[0:1, 361:362]:
                   e.activation(o, i, AF.Ln, bias=1e-08),
                   reads=[("SMALL", 361)], writes=[("SMALL", 362)], tag="l1f_cl")
            p.emit("scalar", lambda e, o=sm[0:1, 363:364], i=sm[0:1, 362:363]:
                   e.activation(o, i, AF.Exp, scale=0.5),
                   reads=[("SMALL", 362)], writes=[("SMALL", 363)], tag="l1f_ce")
            p.emit("vector", lambda e, o=BROW.ap((64, 65), parts=(0, 1)),
                   i=sm[0:1, 363:364]:
                   e.tensor_scalar_mul(o, i, -1.0),
                   reads=[("SMALL", 363)], writes=BROW.keys((64, 65)), tag="l1f_cneg")
            mm(ctx, psc[:, 128:129], cb_ap(ctx, "ONES", (0, 128), parts=(0, 1)),
               BROW.ap((64, 65), parts=(0, 1)),
               reads=[CB_KEY] + BROW.keys((64, 65)), writes=pkc, tag="l1f_cbc")
            p.emit("vector", lambda e, o=t["FAR"][:, BIASH.lo:BIASH.lo + 1],
                   i=psc[:, 128:129]:
                   e.tensor_copy(o, i), reads=pkc, writes=BIASH.keys(), tag="l1f_cb")
            for srcb, dstb in ((QTOK, QPT), (KTOK, KPT)):
                for half in range(2):
                    pk = pst_keys(ctx)
                    for o8 in range(8):
                        ti = half * 8 + o8
                        tr(ctx, pst[0:64, o8 * 128:(o8 + 1) * 128],
                           srcb.ap((ti * 64, (ti + 1) * 64)),
                           reads=srcb.keys((ti * 64, (ti + 1) * 64)), writes=pk,
                           tag="l1f_qct")
                    p.emit("vector", lambda e,
                           o=dstb.ap((half * 1024, (half + 1) * 1024), parts=sp),
                           i=pst[0:64, 0:1024]:
                           e.tensor_copy(o, i),
                           reads=pk, writes=dstb.keys((half * 1024, (half + 1) * 1024)),
                           tag="l1f_qce")
            p.emit("vector", lambda e, o=bass.AP(t["BAR"], VE1.lo + sub * 1056 + 64,
                                                 [[BAR_COLS, 128], [66, 16], [1, 2]]):
                   e.memset(o, 1.0), writes=VE1.keys(), tag="l1f_ones")
            for half in range(2):
                pk = pst_keys(ctx)
                for o8 in range(8):
                    ti = half * 8 + o8
                    tr(ctx, pst[:, o8 * 64:(o8 + 1) * 64],
                       VP.ap((ti * 128, (ti + 1) * 128), parts=sp),
                       reads=VP.keys((ti * 128, (ti + 1) * 128)), writes=pk,
                       psize=64, pbase=base, tag="l1f_vt")
                for o8 in range(8):
                    ti = half * 8 + o8
                    p.emit("vector", lambda e,
                           o=bass.AP(t["BAR"], VE1.lo + sub * 1056 + ti * 66,
                                     [[BAR_COLS, 128], [1, 64]]),
                           i=pst[:, o8 * 64:(o8 + 1) * 64],
                           s=t["FAR"][:, RSTD1.lo + ti:RSTD1.lo + ti + 1]:
                           e.tensor_scalar_mul(o, i, s),
                           reads=pk + RSTD1.keys(), writes=VE1.keys(), tag="l1f_vev")
            for quarter in range(4):
                lastk = quarter * 4 + 3
                pob = psk("PSB", (quarter % 2) * 512, (quarter % 2) * 512 + 264)

                def emit_smm(tk):
                    pks = psk("PSA", (tk % 4) * 512, ((tk % 4) + 1) * 512)
                    mm(ctx, psa[:, (tk % 4) * 512:((tk % 4) + 1) * 512],
                       KPT.ap((tk * 128, (tk + 1) * 128), parts=sp),
                       QPT.ap((quarter * 512, (quarter + 1) * 512), parts=sp),
                       reads=KPT.keys((tk * 128, (tk + 1) * 128)) +
                             QPT.keys((quarter * 512, (quarter + 1) * 512)),
                       writes=pks, tag="l1f_smm")
                    return pks

                pks_q = {0: emit_smm(0)}
                for tk in range(lastk + 1):
                    if tk + 1 <= lastk:
                        pks_q[tk + 1] = emit_smm(tk + 1)
                    pks = pks_q.pop(tk)
                    PTSP = PTS1 if tk % 2 == 0 else PTS1B
                    p.emit("scalar", lambda e, o=PTSP.ap(),
                           i=psa[:, (tk % 4) * 512:((tk % 4) + 1) * 512],
                           b=t["FAR"][:, BIASH.lo:BIASH.lo + 1]:
                           e.activation(o, i, AF.Exp, scale=0.125, bias=b),
                           reads=pks + BIASH.keys(), writes=PTSP.keys(), tag="l1f_exp")
                    subs = [s4 for s4 in range(4) if quarter * 4 + s4 >= tk]
                    if quarter * 4 <= tk:
                        dsub = tk - quarter * 4
                        p.emit("vector", lambda e, o=PTSP.ap((dsub * 128, (dsub + 1) * 128)),
                               i=PTSP.ap((dsub * 128, (dsub + 1) * 128)),
                               m=cb_ap(ctx, "UTI"):
                               e.tensor_mul(o, i, m),
                               reads=PTSP.keys() + [CB_KEY], writes=PTSP.keys(),
                               tag="l1f_dmask")
                    for s4 in subs:
                        mm(ctx, psb[:, (quarter % 2) * 512 + s4 * 66:
                                    (quarter % 2) * 512 + s4 * 66 + 66],
                           PTSP.ap((s4 * 128, (s4 + 1) * 128)),
                           bass.AP(t["BAR"], VE1.lo + sub * 1056 + tk * 66,
                                   [[BAR_COLS, 128], [1, 66]]),
                           reads=PTSP.keys() + VE1.keys(),
                           writes=pob, start=(tk == 0 and s4 == 0),
                           stop=(tk == lastk and s4 == 3), tag="l1f_av")
                p.emit("vector", lambda e, o=sm[:, 372:376],
                       i=bass.AP(t["PSB"], (quarter % 2) * 512 + 64, [[1024, 128], [66, 4]]):
                       e.tensor_copy(o, i), reads=pob,
                       writes=[("SMALL", 372 + i) for i in range(4)], tag="l1f_rcp")
                p.emit("vector", lambda e, o=sm[:, 364:368], i=sm[:, 372:376]:
                       e.reciprocal(o, i), reads=[("SMALL", 372 + i) for i in range(4)],
                       writes=[("SMALL", 364 + i) for i in range(4)], tag="l1f_rc")
                for s4 in range(4):
                    rcol = 364 + s4
                    ti = quarter * 4 + s4
                    p.emit("scalar", lambda e,
                           o=AOB.ap((ti * C + h * 64, ti * C + (h + 1) * 64)),
                           i=psb[:, (quarter % 2) * 512 + s4 * 66:
                                 (quarter % 2) * 512 + s4 * 66 + 64],
                           s=sm[:, rcol:rcol + 1]:
                           e.activation(o, i, AF.Identity, scale=s),
                           reads=pob + [("SMALL", rcol)],
                           writes=AOB.keys((ti * C + h * 64, ti * C + (h + 1) * 64)),
                           tag="l1f_oev")

    dbg(ctx, "aob0", AOB.ap((0, 2048)), AOB.keys((0, 2048)), cast=True)
    dbg(ctx, "qpt15", QPT.ap((0, 2048)), QPT.keys((0, 2048)), cast=True)
    dbg(ctx, "kpt15", KPT.ap((0, 2048)), KPT.keys((0, 2048)), cast=True)
    AOT = ctx.bar(32768, 49152)
    WOT1 = ctx.bar(49152, 57344)
    for cj in range(NCH):
        for half in range(2):
            pk = pst_keys(ctx)
            for tq in range(8):
                ti = half * 8 + tq
                in_lo = ti * C + cj * 128
                tr(ctx, pst[:, tq * 128:(tq + 1) * 128],
                   AOB.ap((in_lo, in_lo + 128)),
                   reads=AOB.keys((in_lo, in_lo + 128)), writes=pk, tag="l1f_aot")
            dst_lo = cj * T + half * 1024
            p.emit("vector", lambda e, o=AOT.ap((dst_lo, dst_lo + 1024)),
                   i=pst[:, 0:1024]:
                   e.tensor_copy(o, i),
                   reads=pk, writes=AOT.keys((dst_lo, dst_lo + 1024)), tag="l1f_aoe")
    WNAT1 = ctx.bar(47680, 53824)
    WNAT1 = ctx.bar(40960, 47144)
    WNAT1 = ctx.bar(16384, 24576)
    src2 = t["L1_ffn_wo"][:, :].rearrange("(a p) c -> p a c", p=128)
    p.emit("gpsimd", lambda g, o=WNAT1.ap((0, 8192), rearr="p (a c) -> p a c", c=C),
           i=src2:
           g.dma_start(o, i), writes=WNAT1.keys((0, 8192)), dma=True, tag="l1f_wold")
    transpose_weight(ctx, Buf(t["BAR"], "BAR", WNAT1.lo, WNAT1.lo + 8192), WOT1, C, C,
                     "l1f_wot")
    if "x3s" not in t:
        t["x3s"] = nc.dram_tensor("x3s", [T, C], F32)
    p.emit("sync", lambda g: g.dma_start(
        t["x3s"][:, :].rearrange("(a p) c -> p a c", p=128),
        xm.ap(rearr="p (a c) -> p a c", c=C)),
        reads=xm.keys(), writes=[("x3s", 0)], dma=True, tag="l1f_x3save")
    for ti in range(NT):
        pbk = psk("PSB", 0, 1024)
        for half in range(2):
            for a in range(NCH):
                mm(ctx, psb[:, half * 512:(half + 1) * 512],
                   AOT.ap((a * T + ti * 128, a * T + (ti + 1) * 128)),
                   WOT1.ap((a * C + half * 512, a * C + (half + 1) * 512)),
                   reads=AOT.keys((a * T + ti * 128, a * T + (ti + 1) * 128)) +
                         WOT1.keys((a * C + half * 512, a * C + (half + 1) * 512)),
                   writes=pbk, start=(a == 0), stop=(a == NCH - 1), tag="l1f_out")
        p.emit("vector", lambda e, o=xm.ap((ti * C + 256, (ti + 1) * C)),
               i=psb[:, 256:1024], x=xm.ap((ti * C + 256, (ti + 1) * C)):
               e.scalar_tensor_tensor(o, i, 1.0, x, ALU.mult, ALU.add),
               reads=pbk + xm.keys((ti * C + 256, (ti + 1) * C)) + [("x3s", 0)],
               writes=xm.keys((ti * C + 256, (ti + 1) * C)), tag="l1f_res")

    dbg(ctx, "x2t0", Buf(t["FAR"], "FAR", 0, 16384).ap((0, 2048)),
        xm.keys((0, 2048)))
    SQS = ctx.bar(58368, 59392)
    GWSB = ctx.far(17633, 17637)
    GWN = ctx.far(17664 - 26, 17664 - 22) if False else None
    GWNAT = ctx.far(17637, 17641)
    GWF = ctx.far(16384, 17408)
    p.emit("sync", lambda g: g.dma_start(t["FAR"][0:4, GWF.lo:GWF.lo + 1024],
                                         t["L1_ffn_gate_w"][:, :]),
           writes=GWF.keys(), dma=True, tag="l1f_gwld")
    pkc = psk("PSC", 0, 512)
    for a in range(NCH):
        tr(ctx, psc[0:128, a * 4:(a + 1) * 4],
           t["FAR"][0:4, GWF.lo + a * 128:GWF.lo + (a + 1) * 128],
           reads=GWF.keys(), writes=pkc, f32=True, psize=4, tag="l1f_gtr")
    p.emit("vector", lambda e, o=t["FAR"][:, GWT.lo:GWT.lo + 32], i=psc[:, 0:32],
           s=bass.AP(t["VEC"], VI["L1_ffn_n2_w"], [[NCH * NV, 128], [NV, 8], [0, 4]]):
           e.tensor_mul(o, i, s),
           reads=pkc + [VEC_KEY], writes=GWT.keys(), tag="l1f_gwf")
    for ti in range(NT):
        p.emit("scalar", lambda e, o=SQS.ap(), i=xm.ap((ti * C, (ti + 1) * C)),
               so=sm[:, 200 + ti:201 + ti]:
               e.activation(o, i, AF.Square, accum_out=so),
               reads=xm.keys((ti * C, (ti + 1) * C)),
               writes=SQS.keys() + [("SMALL", 200 + ti)], tag="l1f_sq2")
        p.emit("scalar", lambda e, o=sm[:, 216 + ti:217 + ti], i=sm[:, 200 + ti:201 + ti]:
               e.activation(o, i, AF.Ln, bias=float(RMS_EPS), scale=1.0 / 1024),
               reads=[("SMALL", 200 + ti)], writes=[("SMALL", 216 + ti)], tag="l1f_ln2")
        p.emit("scalar", lambda e, o=t["FAR"][:, RSTD1.lo + ti:RSTD1.lo + ti + 1],
               i=sm[:, 216 + ti:217 + ti]:
               e.activation(o, i, AF.Exp, scale=-0.5),
               reads=[("SMALL", 216 + ti)], writes=RSTD1.keys(), tag="l1f_rs2")
    for ti in range(NT):
        pkb = psk("PSB", 0, 1024)
        for a in range(NCH):
            tr(ctx, psb[:, a * 128:(a + 1) * 128],
               Buf(t["FAR"], "FAR", 0, 16384).ap((ti * C + a * 128, ti * C + (a + 1) * 128)),
               reads=xm.keys((ti * C + a * 128, ti * C + (a + 1) * 128)),
               writes=pkb, f32=True, tag="l1f_xtr")
        p.emit("vector", lambda e, o=t["FAR"][:, XTF.lo:XTF.lo + 1024], i=psb[:, 0:1024]:
               e.tensor_copy(o, i), reads=pkb, writes=XTF.keys(), tag="l1f_xte")
        pkc2 = psk("PSC", 0, 512)
        for a in range(NCH):
            mm(ctx, psc[:, 16:20],
               t["FAR"][:, XTF.lo + a * 128:XTF.lo + (a + 1) * 128],
               t["FAR"][:, GWT.lo + a * 4:GWT.lo + (a + 1) * 4],
               reads=XTF.keys() + GWT.keys(), writes=pkc2,
               start=(a == 0), stop=(a == NCH - 1), tag="l1f_lmm")
        p.emit("vector", lambda e, o=sm[:, 8 + ti * 4:12 + ti * 4], i=psc[:, 16:20],
               s=t["FAR"][:, RSTD1.lo + ti:RSTD1.lo + ti + 1]:
               e.tensor_scalar_mul(o, i, s),
               reads=pkc2 + RSTD1.keys(), writes=[("SMALL", 8 + ti * 4 + i) for i in range(4)],
               tag="l1f_lg")

    LGK = [("SMALL", 8 + i) for i in range(64)]
    PEK = [("SMALL", 72 + i) for i in range(64)]
    OHK = [("SMALL", 136 + i) for i in range(64)]
    K16 = lambda b: [("SMALL", b + i) for i in range(16)]
    ap3 = lambda off: bass.AP(t["SMALL"], off, [[512, 128], [4, 16], [1, 4]])
    apb = lambda off: bass.AP(t["SMALL"], off, [[512, 128], [1, 16], [0, 4]])
    p.emit("vector", lambda e, o=sm[:, 240:256], i=ap3(8):
           e.reduce_max(o, i, axis=AX.X, negate=True),
           reads=LGK, writes=K16(240), tag="l1f_lm")
    p.emit("vector", lambda e, o=ap3(8), a=ap3(8), b=apb(240):
           e.tensor_add(o, a, b), reads=LGK + K16(240), writes=LGK, tag="l1f_lsh")
    p.emit("scalar", lambda e, o=sm[:, 72:136], i=sm[:, 8:72]:
           e.activation(o, i, AF.Exp),
           reads=LGK, writes=PEK, tag="l1f_pe")
    p.emit("vector", lambda e, o=sm[:, 256:272], i=ap3(72):
           e.reduce_sum(o, i, axis=AX.X),
           reads=PEK, writes=K16(256), tag="l1f_ps")
    p.emit("vector", lambda e, o=sm[:, 272:288], i=sm[:, 256:272]:
           e.reciprocal(o, i), reads=K16(256), writes=K16(272), tag="l1f_pr")
    p.emit("vector", lambda e, o=ap3(72), a=ap3(72), b=apb(272):
           e.tensor_mul(o, a, b), reads=PEK + K16(272), writes=PEK, tag="l1f_pn")
    p.emit("vector", lambda e, o=sm[:, 288:304], i=ap3(72):
           e.reduce_max(o, i, axis=AX.X, negate=True),
           reads=PEK, writes=K16(288), tag="l1f_t1m")
    p.emit("vector", lambda e, o=ap3(136), a=ap3(72), b=apb(288):
           e.tensor_add(o, a, b), reads=PEK + K16(288), writes=OHK, tag="l1f_t1d")
    p.emit("scalar", lambda e, o=sm[:, 136:200], i=sm[:, 136:200]:
           e.activation(o, i, AF.Sign, bias=1e-9),
           reads=OHK, writes=OHK, tag="l1f_t1s")
    p.emit("scalar", lambda e, o=sm[:, 136:200], i=sm[:, 136:200]:
           e.activation(o, i, AF.Identity, scale=0.5, bias=0.5),
           reads=OHK, writes=OHK, tag="l1f_t1o")
    p.emit("vector", lambda e, o=sm[:, 8:72], i=sm[:, 136:200], d=sm[:, 72:136]:
           e.scalar_tensor_tensor(o, i, -4.0, d, ALU.mult, ALU.add),
           reads=OHK + PEK, writes=LGK, tag="l1f_t2p")
    p.emit("vector", lambda e, o=sm[:, 304:320], i=ap3(8):
           e.reduce_max(o, i, axis=AX.X, negate=True),
           reads=LGK, writes=K16(304), tag="l1f_t2m")
    p.emit("vector", lambda e, o=ap3(8), a=ap3(8), b=apb(304):
           e.tensor_add(o, a, b), reads=LGK + K16(304), writes=LGK, tag="l1f_t2d")
    p.emit("scalar", lambda e, o=sm[:, 8:72], i=sm[:, 8:72]:
           e.activation(o, i, AF.Sign, bias=1e-9),
           reads=LGK, writes=LGK, tag="l1f_t2s")
    p.emit("scalar", lambda e, o=sm[:, 8:72], i=sm[:, 8:72]:
           e.activation(o, i, AF.Identity, scale=0.5, bias=0.5),
           reads=LGK, writes=LGK, tag="l1f_t2o")
    p.emit("vector", lambda e, o=sm[:, 320:336], a=sm[:, 288:304], b=sm[:, 304:320]:
           e.tensor_add(o, a, b), reads=K16(288) + K16(304), writes=K16(320),
           tag="l1f_wden")
    p.emit("scalar", lambda e, o=sm[:, 320:336], i=sm[:, 320:336]:
           e.activation(o, i, AF.Identity, scale=-1.0, bias=1e-08),
           reads=K16(320), writes=K16(320), tag="l1f_wd2")
    p.emit("vector", lambda e, o=sm[:, 336:352], i=sm[:, 320:336]:
           e.reciprocal(o, i), reads=K16(320), writes=K16(336), tag="l1f_wr")
    p.emit("vector", lambda e, o=sm[:, 352:368], i=sm[:, 288:304], b=sm[:, 336:352]:
           e.scalar_tensor_tensor(o, i, -1.0, b, ALU.mult, ALU.mult),
           reads=K16(288) + K16(336), writes=K16(352), tag="l1f_w1")
    p.emit("vector", lambda e, o=sm[:, 368:384], i=sm[:, 304:320], b=sm[:, 336:352]:
           e.scalar_tensor_tensor(o, i, -1.0, b, ALU.mult, ALU.mult),
           reads=K16(304) + K16(336), writes=K16(368), tag="l1f_w2")
    wts3 = bass.AP(t["FAR"], WTS.lo, [[FAR_COLS, 128], [4, 16], [1, 4]])
    p.emit("vector", lambda e, o=wts3, a=ap3(136), b=apb(352):
           e.tensor_mul(o, a, b), reads=OHK + K16(352), writes=WTS.keys(),
           tag="l1f_wtsa")
    p.emit("vector", lambda e, o=ap3(8), a=ap3(8), b=apb(368):
           e.tensor_mul(o, a, b), reads=LGK + K16(368), writes=LGK, tag="l1f_wtsb1")
    p.emit("vector", lambda e, o=wts3, a=wts3, b=ap3(8):
           e.tensor_add(o, a, b), reads=WTS.keys() + LGK, writes=WTS.keys(),
           tag="l1f_wtsb")

    dbg(ctx, "wts", t["FAR"][:, WTS.lo:WTS.lo + 64], WTS.keys())
    XFT = ctx.bar(0, 8192)
    NATA = ctx.bar(8192, 9216)
    NATB = ctx.bar(9216, 10240)
    WSLA = ctx.bar(10240, 11264)
    WSLB = ctx.bar(11264, 12288)
    TMPE = ctx.bar(12288, 13312)
    XFB = ctx.bar(13312, 14336)
    NATC = ctx.bar(14336, 15360)
    NATD = ctx.bar(15360, 16384)
    NATS = (NATA, NATB, NATC, NATD)
    CH1 = ctx.bar(16384, 32768)
    CH2 = ctx.bar(32768, 49152)
    MOE = ctx.bar(49152, 57344)
    SQEP = ctx.bar(57344, 58368)

    def u_dma(unit, nat):
        srcw = t[unit["wname"]][unit["rlo"]:unit["rlo"] + 128,
                                unit["clo"]:unit["clo"] + 1024]
        p.emit("gpsimd", lambda g, o=nat.ap((0, 1024)), i=srcw:
               g.dma_start(o, i), writes=nat.keys((0, 1024)), dma=True,
               tag=unit["tag"] + "l")

    def u_tr(unit, nat, wsl):
        nat3 = bass.AP(t["BAR"], nat.lo, [[BAR_COLS, 128], [128, 8], [1, 128]])
        if unit["kind"] == "up":
            out3 = bass.AP(t["BAR"], wsl.lo, [[BAR_COLS, 128], [128, 8], [1, 128]])
            okeys = wsl.keys((0, 1024))
        else:
            out3, okeys = unit["dst3"], unit["dkeys"]
        p.emit("sync", lambda g, o=out3, i=nat3: g.dma_start_transpose(o, i),
               reads=nat.keys((0, 1024)), writes=okeys, dma=True,
               tag=unit["tag"] + "t")

    def run_u(units):
        n = len(units)
        if n == 0:
            return
        for i in range(min(4, n)):
            u_dma(units[i], NATS[i % 4])
        u_tr(units[0], NATS[0], WSLA)
        for u in range(n):
            if u + 4 < n:
                u_dma(units[u + 4], NATS[u % 4])
            if u + 1 < n:
                u_tr(units[u + 1], NATS[(u + 1) % 4],
                     WSLA if (u + 1) % 2 == 0 else WSLB)
            mmfn = units[u].get("mm")
            if mmfn is not None:
                mmfn(WSLA if u % 2 == 0 else WSLB)

    GCTR = [0]

    def up_units(wname, src_buf, dst_buf, act, tagw, rows=2048, ctr=2048):
        nkh = ctr // 1024
        out = []
        for mi in range(rows // 128):
            goff = (GCTR[0] % 2) * 1024
            GCTR[0] += 1
            for kh in range(nkh):
                def mmfn(wsl, mi=mi, kh=kh, goff=goff, last=(kh == nkh - 1)):
                    for nchk in range(2):
                        pkk = psk("PSA", goff + nchk * 512, goff + (nchk + 1) * 512)
                        for a in range(8):
                            ag = kh * 8 + a
                            mm(ctx, psa[:, goff + nchk * 512:goff + (nchk + 1) * 512],
                               wsl.ap((a * 128, (a + 1) * 128)),
                               src_buf.ap((ag * 1024 + nchk * 512,
                                           ag * 1024 + (nchk + 1) * 512)),
                               reads=wsl.keys((a * 128, (a + 1) * 128)) +
                                     src_buf.keys((ag * 1024 + nchk * 512,
                                                   ag * 1024 + (nchk + 1) * 512)),
                               writes=pkk, start=(kh == 0 and a == 0),
                               stop=(last and a == 7), tag=tagw + "m")
                        if not last:
                            continue
                        dlo = mi * 1024 + nchk * 512
                        if act == "gelu" or act == "silu":
                            af = AF.Gelu if act == "gelu" else AF.Silu
                            p.emit("scalar", lambda e, o=dst_buf.ap((dlo, dlo + 512)),
                                   i=psa[:, goff + nchk * 512:goff + (nchk + 1) * 512],
                                   af=af:
                                   e.activation(o, i, af),
                                   reads=pkk, writes=dst_buf.keys((dlo, dlo + 512)),
                                   tag=tagw + "g")
                        else:
                            p.emit("vector", lambda e, o=dst_buf.ap((dlo, dlo + 512)),
                                   i=psa[:, goff + nchk * 512:goff + (nchk + 1) * 512]:
                                   e.tensor_copy(o, i),
                                   reads=pkk, writes=dst_buf.keys((dlo, dlo + 512)),
                                   tag=tagw + "g")
                out.append({"kind": "up", "wname": wname, "rlo": mi * 128,
                            "clo": kh * 1024, "tag": tagw, "mm": mmfn})
        return out

    def fil_units(wname, dst, tagw, chunk_lo=None):
        out = []
        for hh in range(2):
            for ci in range(8):
                base = (dst.lo + (hh * 8 + ci) * 1024 if chunk_lo is None
                        else chunk_lo(hh, ci))
                dst3 = bass.AP(t["BAR"], base,
                               [[BAR_COLS, 128], [128, 8], [1, 128]])
                dkeys = [("BAR", base // 1024)]
                out.append({"kind": "fil", "wname": wname, "rlo": ci * 128,
                            "clo": hh * 1024, "tag": tagw, "mm": None,
                            "dst3": dst3, "dkeys": dkeys})
        return out

    for H in range(2):
        hof = H * 1024
        for ti8 in range(8):
            ti = H * 8 + ti8
            p.emit("scalar", lambda e, o=XFB.ap(), i=xm.ap((ti * C, (ti + 1) * C)),
                   s=t["FAR"][:, RSTD1.lo + ti:RSTD1.lo + ti + 1]:
                   e.activation(o, i, AF.Identity, scale=s),
                   reads=xm.keys((ti * C, (ti + 1) * C)) + RSTD1.keys(),
                   writes=XFB.keys(), tag="l1f_xf")
            pk = pst_keys(ctx)
            for a in range(NCH):
                tr(ctx, pst[:, a * 128:(a + 1) * 128], XFB.ap((a * 128, (a + 1) * 128)),
                   reads=XFB.keys(), writes=pk, tag="l1f_xft")
            p.emit("vector", lambda e,
                   o=bass.AP(t["BAR"], XFT.lo + ti8 * 128,
                             [[BAR_COLS, 128], [1024, 8], [1, 128]]),
                   i=pst[:, 0:1024],
                   s=bass.AP(t["VEC"], VI["L1_ffn_n2_w"],
                             [[NCH * NV, 128], [NV, 8], [0, 128]]):
                   e.tensor_mul(o, i, s),
                   reads=pk + [VEC_KEY], writes=XFT.keys(), tag="l1f_xfe")
        if H == 0:
            dbg(ctx, "xft0", XFT.ap((0, 2048)), XFT.keys((0, 2048)), cast=True)
        for ei, chain in enumerate((
                ("L1_ffn_e0_up", "gelu", "L1_ffn_e0_down"),
                ("L1_ffn_e1_w1", "glu", "L1_ffn_e1_down"),
                ("L1_ffn_e2_l1", "chain2", "L1_ffn_e2_l4"),
                ("L1_ffn_e3_down", "chain3", "L1_ffn_e3_out"))):
            upname, kind, downname = chain

            if kind == "gelu":
                gsrc, wdst = CH1, CH2
                ups = up_units("L1_ffn_e0_up", XFT, CH1, "gelu", "l1f_e0u",
                               ctr=C)
                fils = fil_units(downname, wdst, "l1f_dn")
                units = []
                for i in range(16):
                    units.append(ups[i])
                    units.append(fils[i])
                run_u(units)
                if H == 0:
                    dbg(ctx, "e0u0", CH1.ap((0, 2048)), CH1.keys((0, 2048)), cast=True)
            elif kind == "glu":
                gsrc, wdst = CH1, CH2
                run_u(up_units("L1_ffn_e1_w1", XFT, CH1, "silu", "l1f_e1a", ctr=C) +
                      up_units("L1_ffn_e1_w2", XFT, CH2, "copy", "l1f_e1b", ctr=C))
                for seg in range(16):
                    p.emit("vector", lambda e, o=CH1.ap((seg * 1024, (seg + 1) * 1024)),
                           a=CH1.ap((seg * 1024, (seg + 1) * 1024)),
                           b=CH2.ap((seg * 1024, (seg + 1) * 1024)):
                           e.tensor_mul(o, a, b),
                           reads=CH1.keys((seg * 1024, (seg + 1) * 1024)) +
                                 CH2.keys((seg * 1024, (seg + 1) * 1024)),
                           writes=CH1.keys((seg * 1024, (seg + 1) * 1024)), tag="l1f_e1m")
                run_u(fil_units(downname, wdst, "l1f_dn"))
            elif kind == "chain2":
                gsrc, wdst = CH1, CH2
                run_u(up_units("L1_ffn_e2_l1", XFT, CH1, "gelu", "l1f_e2a", ctr=C) +
                      up_units("L1_ffn_e2_l2", CH1, CH2, "gelu", "l1f_e2b") +
                      up_units("L1_ffn_e2_l3", CH2, CH1, "gelu", "l1f_e2c"))
                run_u(fil_units(downname, wdst, "l1f_dn"))
            else:
                gsrc, wdst = CH2, CH1
                e3lo = lambda hh, ci: CH1.lo + (8192 if hh == 0 else 0) + ci * 1024
                fils = fil_units(downname, wdst, "l1f_dn", chunk_lo=e3lo)
                ups2 = up_units("L1_ffn_e3_up", CH1, CH2, "gelu", "l1f_e3b", ctr=C)
                units = up_units("L1_ffn_e3_down", XFT, CH1, "gelu", "l1f_e3a",
                                 rows=C, ctr=C)
                for i in range(16):
                    units.append(ups2[i])
                    if i % 2 == 0:
                        units.append(fils[i // 2])
                run_u(units)
                run_u(fils[8:])
            if kind == "chain3":
                cbase = lambda hh, half: (CH1.lo + (8192 if hh == 0 else 0) +
                                          half * 4 * 1024)
            else:
                cbase = lambda hh, half: wdst.lo + (hh * 8 + half * 4) * 1024
            for ti8 in range(8):
                ti = H * 8 + ti8
                if ti8 % 2 == 0:
                    pt, pcol = psb, 0
                    pkb = psk("PSB", 0, 1024)
                else:
                    pt, pcol = psa, 0
                    pkb = psk("PSA", 0, 1024)
                for half in range(2):
                    for a in range(16):
                        hh, al = a // 8, a % 8
                        base = cbase(hh, half)
                        mm(ctx, pt[:, pcol + half * 512:pcol + (half + 1) * 512],
                           gsrc.ap((a * 1024 + ti8 * 128, a * 1024 + (ti8 + 1) * 128)),
                           bass.AP(t["BAR"], base + al * 128,
                                   [[BAR_COLS, 128], [1024, 4], [1, 128]]),
                           reads=gsrc.keys((a * 1024 + ti8 * 128, a * 1024 + (ti8 + 1) * 128)) +
                                 [("BAR", base // 1024 + j) for j in range(4)],
                           writes=pkb, start=(a == 0), stop=(a == 15), tag="l1f_om")
                if ei == 0:
                    p.emit("scalar", lambda e, o=MOE.ap((ti8 * C, (ti8 + 1) * C)),
                           i=pt[:, pcol:pcol + 1024],
                           s=t["FAR"][:, WTS.lo + ti * 4 + ei:WTS.lo + ti * 4 + ei + 1]:
                           e.activation(o, i, AF.Identity, scale=s),
                           reads=pkb + WTS.keys(),
                           writes=MOE.keys((ti8 * C, (ti8 + 1) * C)), tag="l1f_moe0")
                else:
                    p.emit("scalar", lambda e, o=TMPE.ap(), i=pt[:, pcol:pcol + 1024],
                           s=t["FAR"][:, WTS.lo + ti * 4 + ei:WTS.lo + ti * 4 + ei + 1]:
                           e.activation(o, i, AF.Identity, scale=s),
                           reads=pkb + WTS.keys(), writes=TMPE.keys(), tag="l1f_moes")
                    p.emit("vector", lambda e, o=MOE.ap((ti8 * C, (ti8 + 1) * C)),
                           a=MOE.ap((ti8 * C, (ti8 + 1) * C)), b=TMPE.ap():
                           e.tensor_add(o, a, b),
                           reads=MOE.keys((ti8 * C, (ti8 + 1) * C)) + TMPE.keys(),
                           writes=MOE.keys((ti8 * C, (ti8 + 1) * C)), tag="l1f_moea")
        if H == 0:
            dbg(ctx, "moeH0", MOE.ap((0, 2048)), MOE.keys((0, 2048)), cast=True)
        ZF2 = ctx.far(16384, 17408)
        X3F = ctx.far(17664, 18688)
        for ti8 in range(8):
            ti = H * 8 + ti8
            zk = ZF2.keys()
            xk = xm.keys((ti * C, (ti + 1) * C))
            p.emit("sync", lambda g, o=t["FAR"][:, X3F.lo:X3F.lo + 1024],
                   i=t["x3s"][ti * 128:(ti + 1) * 128, :]:
                   g.dma_start(o, i),
                   reads=[("x3s", 0)], writes=X3F.keys(), dma=True, tag="l1f_x3ld")
            p.emit("vector", lambda e, o=t["FAR"][:, ZF2.lo:ZF2.lo + 1024],
                   a=xm.ap((ti * C, (ti + 1) * C)),
                   b=t["FAR"][:, X3F.lo:X3F.lo + 1024]:
                   e.tensor_add(o, a, b), reads=xk + X3F.keys(), writes=zk,
                   tag="l1f_zc")
            p.emit("vector", lambda e, o=t["FAR"][:, ZF2.lo + 256:ZF2.lo + 1024],
                   a=MOE.ap((ti8 * C + 256, (ti8 + 1) * C)),
                   b=t["FAR"][:, ZF2.lo + 256:ZF2.lo + 1024]:
                   e.tensor_add(o, a, b),
                   reads=MOE.keys((ti8 * C + 256, (ti8 + 1) * C)) + zk, writes=zk,
                   tag="l1f_za")
            p.emit("vector", lambda e, o=sm[:, 0:1], i=t["FAR"][:, ZF2.lo:ZF2.lo + 1024]:
                   e.reduce_sum(o, i, axis=AX.X), reads=zk, writes=[("SMALL", 0)],
                   tag="l1f_zs")
            p.emit("scalar", lambda e, o=SQEP.ap(), i=t["FAR"][:, ZF2.lo:ZF2.lo + 1024],
                   so=sm[:, 1:2]:
                   e.activation(o, i, AF.Square, accum_out=so),
                   reads=zk, writes=SQEP.keys() + [("SMALL", 1)], tag="l1f_zq")
            p.emit("vector", lambda e, o=sm[:, 2:3], i=sm[:, 0:1]:
                   e.tensor_scalar_mul(o, i, 1.0 / C), reads=[("SMALL", 0)],
                   writes=[("SMALL", 2)], tag="l1f_zm")
            p.emit("vector", lambda e, o=sm[:, 3:4], m=sm[:, 2:3]:
                   e.tensor_mul(o, m, m), reads=[("SMALL", 2)], writes=[("SMALL", 3)],
                   tag="l1f_zm2")
            p.emit("vector", lambda e, o=sm[:, 4:5], i=sm[:, 1:2], m2=sm[:, 3:4]:
                   e.scalar_tensor_tensor(o, i, 1.0 / C, m2, ALU.mult, ALU.subtract),
                   reads=[("SMALL", 1), ("SMALL", 3)], writes=[("SMALL", 4)], tag="l1f_zv")
            p.emit("scalar", lambda e, o=sm[:, 5:6], i=sm[:, 4:5]:
                   e.activation(o, i, AF.Ln, bias=float(LN_EPS)),
                   reads=[("SMALL", 4)], writes=[("SMALL", 5)], tag="l1f_zl")
            p.emit("scalar", lambda e, o=sm[:, 6:7], i=sm[:, 5:6]:
                   e.activation(o, i, AF.Exp, scale=-0.5),
                   reads=[("SMALL", 5)], writes=[("SMALL", 6)], tag="l1f_ze")
            p.emit("vector", lambda e, o=sm[:, 7:8], m=sm[:, 2:3], r=sm[:, 6:7]:
                   e.scalar_tensor_tensor(o, m, -1.0, r, ALU.mult, ALU.mult),
                   reads=[("SMALL", 2), ("SMALL", 6)], writes=[("SMALL", 7)], tag="l1f_zb")
            p.emit("scalar", lambda e, o=xm.ap((ti * C, (ti + 1) * C)),
                   i=t["FAR"][:, ZF2.lo:ZF2.lo + 1024],
                   sc=sm[:, 6:7], b=sm[:, 7:8]:
                   e.activation(o, i, AF.Identity, bias=b, scale=sc),
                   reads=zk + [("SMALL", 6), ("SMALL", 7)], writes=xk, tag="l1f_zap")





L0A_UPTO = 99
L0A_SUBUPTO = 99
L0A_PART = 99


def stage_l0a(ctx, xm):
    p, nc, t = ctx.p, ctx.nc, ctx.t
    psa, psb, psc, pst = t["PSA"], t["PSB"], t["PSC"], t["PST"]
    sm = t["SMALL"]

    for nm, shape in (("L0_attn_W_r", (C, C)), ("L0_attn_W_k", (C, C)),
                      ("L0_attn_W_v", (C, C)), ("L0_attn_W_o", (C, C)),
                      ("L0_attn_time_maa_w1", (C, 128)),
                      ("L0_attn_time_decay_w1", (C, 64)),
                      ("L0_attn_time_decay_w2", (64, C))):
        ctx.dram(nm, shape)
    t["w2_lhsT_d"] = nc.dram_tensor("w2_lhsT_d", [128, 4 * C], F32, kind="ExternalInput")

    WR = ctx.bar(0, 8192)
    WK = ctx.bar(8192, 16384)
    WV = ctx.bar(16384, 24576)
    WO = ctx.bar(24576, 32768)
    W1L = ctx.bar(32768, 33792)
    W2L = ctx.bar(33792, 37888)
    DW1 = ctx.bar(37888, 38400)
    DW2 = ctx.bar(38400, 39424)
    XTC = ctx.bar(39424, 41488)
    XBC = ctx.bar(41488, 41744)
    MIX = ctx.bar(42776 - 1032, 44824 - 1032) if False else ctx.bar(42776, 44824)
    KHT = ctx.bar(42776, 43800)
    SB16 = ctx.bar(43800, 44312)
    XXC = ctx.bar(44824, 46872)
    RSC = ctx.bar(44824, 45848)
    KSC = ctx.bar(45848, 46872)
    RC = ctx.bar(46872, 48920)
    KC = ctx.bar(48920, 50968)
    VC = ctx.bar(50968, 53016)
    EWWC = ctx.bar(53016, 55064)
    H1 = ctx.bar(55064, 55320)
    MT = ctx.bar(55320, 55576)
    SCR1 = ctx.bar(55576, 56600)
    KHC = ctx.bar(56600, 57624)
    AM = ctx.bar(57624, 58648)
    Y1 = ctx.bar(58648, 59672)

    ZF = ctx.far(16384, 17416)
    SF = ctx.far(17416, 17928)
    DCF = ctx.far(17928, 17944)
    NLWE = ctx.far(17944, 17952)
    EWEND = ctx.far(17952, 17960)
    MNB = ctx.far(17960, 17976)
    RSB = ctx.far(17976, 17992)
    SQEP = XBC
    SQ2 = ctx.bar(41744, 42768)

    MIX = ctx.bar(41744, 43792)
    KHT = ctx.bar(42776, 43800) if False else None
    KHT = ctx.bar(41744, 42768)
    SB16 = ctx.bar(42768, 43280)
    EPSQ = ctx.bar(58648, 59672)
    gam = None

    for wname, wb in (("L0_attn_W_r", WR), ("L0_attn_W_k", WK),
                      ("L0_attn_W_v", WV), ("L0_attn_W_o", WO)):
        srcw = t[wname][:, :].rearrange("(a p) c -> p a c", p=128)
        p.emit("gpsimd", lambda g, o=wb.ap(rearr="p (a c) -> p a c", c=C), i=srcw:
               g.dma_start(o, i), writes=wb.keys(), dma=True, tag="l0a_wld")
    p.emit("gpsimd", lambda g, o=W1L.ap(rearr="p (a c) -> p a c", c=128),
           i=t["L0_attn_time_maa_w1"][:, :].rearrange("(a p) c -> p a c", p=128):
           g.dma_start(o, i), writes=W1L.keys(), dma=True, tag="l0a_w1")
    p.emit("gpsimd", lambda g, o=W2L.ap(), i=t["w2_lhsT_d"][:, :]:
           g.dma_start(o, i), writes=W2L.keys(), dma=True, tag="l0a_w2")
    p.emit("gpsimd", lambda g, o=DW1.ap(rearr="p (a c) -> p a c", c=64),
           i=t["L0_attn_time_decay_w1"][:, :].rearrange("(a p) c -> p a c", p=128):
           g.dma_start(o, i), writes=DW1.keys(), dma=True, tag="l0a_dw1")
    p.emit("gpsimd", lambda g, o=DW2.ap((0, 1024), parts=(0, 64)),
           i=t["L0_attn_time_decay_w2"][:, :]:
           g.dma_start(o, i), writes=DW2.keys(), dma=True, tag="l0a_dw2")
    p.emit("vector", lambda e, o=t["FAR"][:, SF.lo:SF.lo + 512]: e.memset(o, 0.0),
           writes=SF.keys(), tag="l0a_s0")
    p.emit("vector", lambda e, o=SB16.ap(): e.memset(o, 0.0),
           writes=SB16.keys(), tag="l0a_s0b")
    p.emit("vector", lambda e, o=XTC.ap(): e.memset(o, 0.0),
           writes=XTC.keys(), tag="l0a_xz")

    for ci in range(8):
        t0 = ci * 256
        if ci > 0:
            p.emit("vector", lambda e,
                   o=bass.AP(t["BAR"], XTC.lo, [[BAR_COLS, 128], [258, 8], [1, 1]]),
                   i=bass.AP(t["BAR"], XTC.lo + 256, [[BAR_COLS, 128], [258, 8], [1, 1]]):
                   e.tensor_copy(o, i), reads=XTC.keys(), writes=XTC.keys(),
                   tag="l0a_pc")
        for s2 in range(2):
            ti = ci * 2 + s2
            p.emit("vector", lambda e, o=XBC.ap(cols=(0, 128)) if False else XBC.ap(),
                   i=xm.ap((ti * C, ti * C + 256)):
                   e.tensor_copy(o, i), reads=xm.keys((ti * C, ti * C + 256)),
                   writes=XBC.keys(), tag="l0a_cast") if False else None
        for s2 in range(2):
            ti = ci * 2 + s2
            for cj in range(NCH):
                p.emit("vector", lambda e, o=XBC.ap((0, 128)),
                       i=xm.ap((ti * C + cj * 128, ti * C + (cj + 1) * 128)):
                       e.tensor_copy(o, i),
                       reads=xm.keys((ti * C + cj * 128, ti * C + (cj + 1) * 128)),
                       writes=XBC.keys(), tag="l0a_cast")
                pk = pst_keys(ctx)
                tr(ctx, pst[:, 0:128], XBC.ap((0, 128)),
                   reads=XBC.keys(), writes=pk, tag="l0a_xtr")
                dst_lo = cj * 258 + 1 + s2 * 128
                p.emit("vector", lambda e, o=XTC.ap((dst_lo, dst_lo + 128)),
                       i=pst[:, 0:128]:
                       e.tensor_copy(o, i), reads=pk,
                       writes=XTC.keys((dst_lo, dst_lo + 128)), tag="l0a_xte")
        p.emit("vector", lambda e,
               o=bass.AP(t["BAR"], XXC.lo, [[BAR_COLS, 128], [256, 8], [1, 256]]),
               a=bass.AP(t["BAR"], XTC.lo, [[BAR_COLS, 128], [258, 8], [1, 256]]),
               b=bass.AP(t["BAR"], XTC.lo + 1, [[BAR_COLS, 128], [258, 8], [1, 256]]):
               e.tensor_sub(o, a, b),
               reads=XTC.keys(), writes=XXC.keys(), tag="l0a_xx")
        for cj in range(NCH):
            p.emit("vector", lambda e, o=MIX.ap((cj * 256, (cj + 1) * 256)),
                   i=XXC.ap((cj * 256, (cj + 1) * 256)),
                   s=vec_col(ctx, "L0_attn_time_maa_x", cj),
                   x=XTC.ap((cj * 258 + 1, cj * 258 + 257)):
                   e.scalar_tensor_tensor(o, i, s, x, ALU.mult, ALU.add),
                   reads=XXC.keys((cj * 256, (cj + 1) * 256)) + [VEC_KEY] +
                         XTC.keys((cj * 258, (cj + 1) * 258)),
                   writes=MIX.keys((cj * 256, (cj + 1) * 256)), tag="l0a_xxx")
        pkc = psk("PSC", 0, 512)
        for a in range(NCH):
            mm(ctx, psc[:, 0:256], W1L.ap((a * 128, (a + 1) * 128)),
               MIX.ap((a * 256, (a + 1) * 256)),
               reads=W1L.keys() + MIX.keys((a * 256, (a + 1) * 256)),
               writes=pkc, start=(a == 0), stop=(a == NCH - 1), tag="l0a_mmm")
        p.emit("scalar", lambda e, o=MT.ap(), i=psc[:, 0:256]:
               e.activation(o, i, AF.Tanh),
               reads=pkc, writes=MT.keys(), tag="l0a_mt")
        vnames = {"w": "L0_attn_time_maa_w", "k": "L0_attn_time_maa_k",
                  "v": "L0_attn_time_maa_v", "r": "L0_attn_time_maa_r"}
        for fi, which in enumerate(("w", "k", "v", "r")):
            for half in range(2):
                pka = psk("PSA", half * 1024, (half + 1) * 1024)
                for mi4 in range(4):
                    mi = half * 4 + mi4
                    mm(ctx, psa[:, half * 1024 + mi4 * 256: half * 1024 + (mi4 + 1) * 256],
                       W2L.ap((fi * C + mi * 128, fi * C + (mi + 1) * 128)),
                       MT.ap(),
                       reads=W2L.keys((fi * C + mi * 128, fi * C + (mi + 1) * 128)) +
                             MT.keys(),
                       writes=pka, tag="l0a_m2")
                p.emit("vector", lambda e, o=SCR1.ap(),
                       i=psa[:, half * 1024:(half + 1) * 1024],
                       s=bass.AP(t["VEC"], half * 4 * NV + VI[vnames[which]],
                                 [[NCH * NV, 128], [NV, 4], [0, 256]]):
                       e.tensor_add(o, i, s),
                       reads=pka + [VEC_KEY], writes=SCR1.keys(), tag="l0a_mx1")
                p.emit("vector", lambda e, o=SCR1.ap(), a=SCR1.ap(),
                       b=XXC.ap((half * 1024, (half + 1) * 1024)):
                       e.tensor_mul(o, a, b),
                       reads=SCR1.keys() + XXC.keys((half * 1024, (half + 1) * 1024)),
                       writes=SCR1.keys(), tag="l0a_mx2")
                p.emit("vector", lambda e, o=MIX.ap((half * 1024, (half + 1) * 1024)),
                       a=SCR1.ap(),
                       b=bass.AP(t["BAR"], XTC.lo + half * 4 * 258 + 1,
                                 [[BAR_COLS, 128], [258, 4], [1, 256]]):
                       e.tensor_add(o, a, b),
                       reads=SCR1.keys() + XTC.keys(),
                       writes=MIX.keys((half * 1024, (half + 1) * 1024)), tag="l0a_mx3")
            if which in ("r", "k"):
                dstb = RC if which == "r" else KC
                wb = WR if which == "r" else WK
                for half in range(2):
                    pka = psk("PSA", half * 1024, (half + 1) * 1024)
                    for mi4 in range(4):
                        mi = half * 4 + mi4
                        for a in range(NCH):
                            mm(ctx, psa[:, half * 1024 + mi4 * 256:
                                        half * 1024 + (mi4 + 1) * 256],
                               wb.ap((a * C + mi * 128, a * C + (mi + 1) * 128)),
                               MIX.ap((a * 256, (a + 1) * 256)),
                               reads=wb.keys((a * C + mi * 128, a * C + (mi + 1) * 128)) +
                                     MIX.keys((a * 256, (a + 1) * 256)),
                               writes=pka, start=(a == 0), stop=(a == NCH - 1),
                               tag="l0a_pj")
                    p.emit("vector", lambda e,
                           o=dstb.ap((half * 1024, (half + 1) * 1024)),
                           i=psa[:, half * 1024:(half + 1) * 1024]:
                           e.tensor_copy(o, i),
                           reads=pka, writes=dstb.keys((half * 1024, (half + 1) * 1024)),
                           tag="l0a_pje")
            elif which == "v":
                for s2 in range(2):
                    pkb = psk("PSB", 0, 1024)
                    for nh in range(2):
                        for a in range(NCH):
                            mm(ctx, psb[:, nh * 512:(nh + 1) * 512],
                               MIX.ap((a * 256 + s2 * 128, a * 256 + s2 * 128 + 128)),
                               WV.ap((a * C + nh * 512, a * C + (nh + 1) * 512)),
                               reads=MIX.keys((a * 256, (a + 1) * 256)) +
                                     WV.keys((a * C + nh * 512, a * C + (nh + 1) * 512)),
                               writes=pkb, start=(a == 0), stop=(a == NCH - 1),
                               tag="l0a_vp")
                    p.emit("vector", lambda e, o=VC.ap((s2 * 1024, (s2 + 1) * 1024)),
                           i=psb[:, 0:1024]:
                           e.tensor_copy(o, i),
                           reads=pkb, writes=VC.keys((s2 * 1024, (s2 + 1) * 1024)),
                           tag="l0a_vpe")
            else:
                pkc2 = psk("PSC", 0, 512)
                for a in range(NCH):
                    mm(ctx, psc[0:64, 256:512], DW1.ap((a * 64, (a + 1) * 64)),
                       MIX.ap((a * 256, (a + 1) * 256)),
                       reads=DW1.keys() + MIX.keys((a * 256, (a + 1) * 256)),
                       writes=pkc2, start=(a == 0), stop=(a == NCH - 1), tag="l0a_h1")
                p.emit("scalar", lambda e, o=H1.ap(parts=(0, 64)), i=psc[0:64, 256:512]:
                       e.activation(o, i, AF.Tanh),
                       reads=pkc2, writes=H1.keys(), tag="l0a_h1e")
                for mi in range(NCH):
                    pkb = psk("PSB", (mi % 2) * 512, ((mi % 2) + 1) * 512)
                    mm(ctx, psb[:, (mi % 2) * 512:(mi % 2) * 512 + 256],
                       DW2.ap((mi * 128, (mi + 1) * 128), parts=(0, 64)),
                       H1.ap(parts=(0, 64)),
                       reads=DW2.keys() + H1.keys(), writes=pkb, tag="l0a_ww")
                    p.emit("scalar", lambda e, o=EWWC.ap((mi * 256, (mi + 1) * 256)),
                           i=psb[:, (mi % 2) * 512:(mi % 2) * 512 + 256],
                           b=vec_col(ctx, "L0_attn_time_decay", mi):
                           e.activation(o, i, AF.Exp, bias=b),
                           reads=pkb + [VEC_KEY],
                           writes=EWWC.keys((mi * 256, (mi + 1) * 256)), tag="l0a_ewe")
        if ci == 0:
            dbg(ctx, "xtc", XTC.ap((0, 2048)), XTC.keys(), cast=True)
            dbg(ctx, "mixr", MIX.ap((0, 2048)), MIX.keys(), cast=True)
            dbg(ctx, "mt", MT.ap((0, 256)), MT.keys(), cast=True)
            dbg(ctx, "ewwc", EWWC.ap((0, 2048)), EWWC.keys(), cast=True)
            dbg(ctx, "rc", RC.ap((0, 2048)), RC.keys(), cast=True)
            dbg(ctx, "kc", KC.ap((0, 2048)), KC.keys(), cast=True)
            dbg(ctx, "vc", VC.ap((0, 2048)), VC.keys(), cast=True)
        if ci >= L0A_UPTO:
            return
        for s2 in range(2):
            ti = ci * 2 + s2
            if ci == L0A_UPTO - 1 and s2 >= L0A_SUBUPTO:
                return
            LWS = ZF
            p.emit("vector", lambda e,
                   o=bass.AP(t["FAR"], LWS.lo, [[FAR_COLS, 128], [129, 8], [1, 1]]):
                   e.memset(o, 0.0), writes=LWS.keys(), tag="l0a_lz")
            for par in range(2):
                p.emit("vector", lambda e,
                       o=SB16.ap(parts=(par * 64, par * 64 + 64)),
                       i=t["FAR"][par * 64:par * 64 + 64, SF.lo:SF.lo + 512]:
                       e.tensor_copy(o, i),
                       reads=SF.keys(), writes=SB16.keys(), tag="l0a_sbr")
            for cj in range(NCH):
                p.emit("vector", lambda e,
                       o=t["FAR"][:, LWS.lo + cj * 129 + 1:LWS.lo + cj * 129 + 129],
                       d=EWWC.ap((cj * 256 + s2 * 128, cj * 256 + s2 * 128 + 128)):
                       e.tensor_tensor_scan(o, d, d, 0.0, ALU.add, ALU.bypass),
                       reads=EWWC.keys((cj * 256, (cj + 1) * 256)), writes=LWS.keys(),
                       tag="l0a_scan")
            p.emit("vector", lambda e, o=t["FAR"][:, NLWE.lo:NLWE.lo + 8],
                   i=bass.AP(t["FAR"], LWS.lo + 128, [[FAR_COLS, 128], [129, 8], [1, 1]]):
                   e.tensor_scalar_mul(o, i, -1.0),
                   reads=LWS.keys(), writes=NLWE.keys(), tag="l0a_nlwe")
            p.emit("scalar", lambda e, o=t["FAR"][:, EWEND.lo:EWEND.lo + 8],
                   i=t["FAR"][:, NLWE.lo:NLWE.lo + 8]:
                   e.activation(o, i, AF.Exp),
                   reads=NLWE.keys(), writes=EWEND.keys(), tag="l0a_ewend")
            p.emit("scalar", lambda e, o=SCR1.ap(),
                   i=bass.AP(t["FAR"], LWS.lo, [[FAR_COLS, 128], [129, 8], [1, 128]]):
                   e.activation(o, i, AF.Exp, scale=-1.0),
                   reads=LWS.keys(), writes=SCR1.keys(), tag="l0a_elw")
            p.emit("vector", lambda e, o=RSC.ap(), a=SCR1.ap(),
                   b=bass.AP(t["BAR"], RC.lo + s2 * 128, [[BAR_COLS, 128], [256, 8], [1, 128]]):
                   e.tensor_mul(o, a, b),
                   reads=SCR1.keys() + RC.keys(), writes=RSC.keys(), tag="l0a_rs")
            p.emit("scalar", lambda e, o=SCR1.ap(),
                   i=bass.AP(t["FAR"], LWS.lo, [[FAR_COLS, 128], [129, 8], [1, 128]]):
                   e.activation(o, i, AF.Exp),
                   reads=LWS.keys(), writes=SCR1.keys(), tag="l0a_elwn")
            p.emit("vector", lambda e, o=KSC.ap(), a=SCR1.ap(),
                   b=bass.AP(t["BAR"], KC.lo + s2 * 128, [[BAR_COLS, 128], [256, 8], [1, 128]]):
                   e.tensor_mul(o, a, b),
                   reads=SCR1.keys() + KC.keys(), writes=KSC.keys(), tag="l0a_ks")
            for cj in range(NCH):
                p.emit("scalar", lambda e, o=SCR1.ap((cj * 128, (cj + 1) * 128)),
                       i=t["FAR"][:, LWS.lo + cj * 129:LWS.lo + cj * 129 + 128],
                       b=t["FAR"][:, NLWE.lo + cj:NLWE.lo + cj + 1]:
                       e.activation(o, i, AF.Exp, bias=b),
                       reads=LWS.keys() + NLWE.keys(),
                       writes=SCR1.keys((cj * 128, (cj + 1) * 128)), tag="l0a_ekh")
            p.emit("vector", lambda e, o=KHC.ap(), a=SCR1.ap(),
                   b=bass.AP(t["BAR"], KC.lo + s2 * 128, [[BAR_COLS, 128], [256, 8], [1, 128]]):
                   e.tensor_mul(o, a, b),
                   reads=SCR1.keys() + KC.keys(), writes=KHC.keys(), tag="l0a_kh")
            if L0A_PART <= 1:
                return
            for par in range(2):
                base = par * 64
                pk = pst_keys(ctx)
                for j in range(8):
                    cj = j
                    tr(ctx, pst[:, j * 64:(j + 1) * 64],
                       KHC.ap((cj * 128, (cj + 1) * 128), parts=(base, base + 64)),
                       reads=KHC.keys((cj * 128, (cj + 1) * 128)), writes=pk,
                       psize=64, pbase=base, tag="l0a_kht")
                p.emit("vector", lambda e, o=KHT.ap((par * 512, (par + 1) * 512)),
                       i=pst[:, 0:512]:
                       e.tensor_copy(o, i), reads=pk,
                       writes=KHT.keys((par * 512, (par + 1) * 512)), tag="l0a_khe")
            if L0A_PART <= 2:
                return
            for cj in range(NCH):
                p.emit("vector", lambda e, o=SCR1.ap((cj * 128, (cj + 1) * 128)),
                       i=bass.AP(t["BAR"], RC.lo + cj * 256 + s2 * 128,
                                 [[BAR_COLS, 128], [1, 128]]),
                       s=vec_col(ctx, "L0_attn_time_faaaa", cj),
                       d=bass.AP(t["BAR"], KC.lo + cj * 256 + s2 * 128,
                                 [[BAR_COLS, 128], [1, 128]]):
                       e.scalar_tensor_tensor(o, i, s, d, ALU.mult, ALU.mult),
                       reads=RC.keys((cj * 256, (cj + 1) * 256)) + [VEC_KEY] +
                             KC.keys((cj * 256, (cj + 1) * 256)),
                       writes=SCR1.keys((cj * 128, (cj + 1) * 128)), tag="l0a_rku")
            pkd = psk("PSA", 1024, 2048)
            for par in range(2):
                base = par * 64
                for j in range(8):
                    h = 2 * j + par
                    cj = h // 2
                    mm(ctx, psa[:, 1024 + par * 512 + j:1024 + par * 512 + j + 1],
                       SCR1.ap((cj * 128, (cj + 1) * 128), parts=(base, base + 64)),
                       cb_ap(ctx, "ONES", (0, 1), parts=(base, base + 64)),
                       reads=SCR1.keys((cj * 128, (cj + 1) * 128)) + [CB_KEY],
                       writes=pkd, tag="l0a_dg")
            for par in range(2):
                p.emit("vector", lambda e,
                       o=bass.AP(t["FAR"], DCF.lo + par, [[FAR_COLS, 128], [2, 8], [1, 1]]),
                       i=bass.AP(t["PSA"], 1024 + par * 512, [[2048, 128], [1, 8], [1, 1]]):
                       e.tensor_copy(o, i), reads=pkd, writes=DCF.keys(),
                       tag="l0a_dge")
            if L0A_PART <= 3:
                return
            for hg in range(2):
                pka = psk("PSA", 0, 1024)
                for par in range(2):
                    base = par * 64
                    for j in range(4):
                        h = hg * 8 + 2 * j + par
                        cj = h // 2
                        slot = par * 4 + j
                        mm(ctx, psa[:, slot * 128:(slot + 1) * 128],
                           KSC.ap((cj * 128, (cj + 1) * 128), parts=(base, base + 64)),
                           RSC.ap((cj * 128, (cj + 1) * 128), parts=(base, base + 64)),
                           reads=KSC.keys((cj * 128, (cj + 1) * 128)) +
                                 RSC.keys((cj * 128, (cj + 1) * 128)),
                           writes=pka, tag="l0a_amm")
                p.emit("vector", lambda e, o=AM.ap(), i=psa[:, 0:1024],
                       m=bass.AP(t["CB"], _CONSTS["cb_ofs"]["UTS"][0],
                                 [[4624, 128], [0, 8], [1, 128]]):
                       e.tensor_mul(o, i, m),
                       reads=pka + [CB_KEY], writes=AM.keys(), tag="l0a_amk")
                if L0A_PART <= 4:
                    continue
                pkb = psk("PSB", 0, 1024)
                for h8 in range(8):
                    h = hg * 8 + h8
                    cj = h // 2
                    base = (h % 2) * 64
                    slot = (h8 % 2) * 4 + h8 // 2
                    mm(ctx, psb[:, h * 64:(h + 1) * 64],
                       AM.ap((slot * 128, (slot + 1) * 128)),
                       VC.ap((s2 * 1024 + h * 64, s2 * 1024 + (h + 1) * 64)),
                       reads=AM.keys((slot * 128, (slot + 1) * 128)) +
                             VC.keys((s2 * 1024, (s2 + 1) * 1024)),
                       writes=pkb, start=(h == 0 or h == 8), stop=False, tag="l0a_yi")
                    mm(ctx, psb[:, h * 64:(h + 1) * 64],
                       RSC.ap((cj * 128, (cj + 1) * 128), parts=(base, base + 64)),
                       SB16.ap(((h // 2) * 64, (h // 2) * 64 + 64), parts=(base, base + 64)),
                       reads=RSC.keys((cj * 128, (cj + 1) * 128)) + SB16.keys(),
                       writes=pkb, start=False,
                       stop=(h == 7 or h == HEADS - 1), tag="l0a_yc")
            if L0A_PART <= 5:
                return
            p.emit("vector", lambda e, o=Y1.ap(),
                   a=VC.ap((s2 * 1024, (s2 + 1) * 1024)),
                   b=bass.AP(t["FAR"], DCF.lo, [[FAR_COLS, 128], [1, 16], [0, 64]]):
                   e.tensor_mul(o, a, b),
                   reads=VC.keys((s2 * 1024, (s2 + 1) * 1024)) + DCF.keys(),
                   writes=Y1.keys(), tag="l0a_yd")
            p.emit("vector", lambda e, o=psb[:, 0:1024], a=Y1.ap(), b=psb[:, 0:1024]:
                   e.tensor_add(o, a, b),
                   reads=Y1.keys() + psk("PSB", 0, 1024), writes=psk("PSB", 0, 1024),
                   tag="l0a_yda")
            if ci == 0 and s2 == 0:
                dbg(ctx, "lws", t["FAR"][:, ZF.lo:ZF.lo + 1032], ZF.keys())
                dbg(ctx, "rsc", RSC.ap(), RSC.keys(), cast=True)
                dbg(ctx, "ksc", KSC.ap(), KSC.keys(), cast=True)
                dbg(ctx, "am", AM.ap(), AM.keys(), cast=True)
                dbg(ctx, "dcf", t["FAR"][:, DCF.lo:DCF.lo + 16], DCF.keys())
            if ci == 0 and s2 == 0 and DEBUG:
                p.emit("vector", lambda e, o=SCR1.ap(), i=psb[:, 0:1024]:
                       e.tensor_copy(o, i),
                       reads=psk("PSB", 0, 1024), writes=SCR1.keys(), tag="dbgyp")
                dbg(ctx, "ypsum", SCR1.ap(), SCR1.keys(), cast=True)
            if L0A_PART <= 6:
                return
            for par in range(2):
                pkq = psk("PSC", 0, 512)
                for h2 in range(8):
                    h = h2 * 2 + par
                    mm(ctx, psc[par * 64:par * 64 + 64, h2 * 64:(h2 + 1) * 64],
                       KHT.ap((par * 512 + h2 * 64, par * 512 + (h2 + 1) * 64)),
                       VC.ap((s2 * 1024 + h * 64, s2 * 1024 + (h + 1) * 64)),
                       reads=KHT.keys((h * 64, (h + 1) * 64)) +
                             VC.keys((s2 * 1024, (s2 + 1) * 1024)),
                       writes=pkq, tile_position=(0, par * 64), tag="l0a_ds")
                sl = t["FAR"][par * 64:par * 64 + 64, EWEND.lo:EWEND.lo + 8]
                ewb = bass.AP(sl.tensor, sl.offset, [[FAR_COLS, 64], [1, 8], [0, 64]])
                p.emit("vector", lambda e,
                       o=t["FAR"][par * 64:par * 64 + 64, SF.lo:SF.lo + 512],
                       a=t["FAR"][par * 64:par * 64 + 64, SF.lo:SF.lo + 512],
                       b=ewb:
                       e.tensor_mul(o, a, b),
                       reads=SF.keys() + EWEND.keys(), writes=SF.keys(), tag="l0a_sg")
                p.emit("vector", lambda e,
                       o=t["FAR"][par * 64:par * 64 + 64, SF.lo:SF.lo + 512],
                       a=psc[par * 64:par * 64 + 64, 0:512],
                       b=t["FAR"][par * 64:par * 64 + 64, SF.lo:SF.lo + 512]:
                       e.tensor_add(o, a, b),
                       reads=pkq + SF.keys(), writes=SF.keys(), tag="l0a_su")
            if L0A_PART <= 7:
                return
            p.emit("vector", lambda e, o=t["FAR"][:, MNB.lo:MNB.lo + 16],
                   i=psb[:, 0:1024].rearrange("p (a d) -> p a d", d=64):
                   e.reduce_sum(o, i, axis=AX.X),
                   reads=psk("PSB", 0, 1024), writes=MNB.keys(), tag="l0a_gs")
            p.emit("scalar", lambda e, o=Y1.ap(), i=psb[:, 0:1024]:
                   e.activation(o, i, AF.Square),
                   reads=psk("PSB", 0, 1024), writes=Y1.keys(), tag="l0a_gq")
            p.emit("vector", lambda e, o=t["FAR"][:, RSB.lo:RSB.lo + 16],
                   i=Y1.ap(rearr="p (a d) -> p a d", d=64):
                   e.reduce_sum(o, i, axis=AX.X),
                   reads=Y1.keys(), writes=RSB.keys(), tag="l0a_gq2")
            p.emit("vector", lambda e, o=t["FAR"][:, MNB.lo:MNB.lo + 16],
                   i=t["FAR"][:, MNB.lo:MNB.lo + 16]:
                   e.tensor_scalar_mul(o, i, 1.0 / 64),
                   reads=MNB.keys(), writes=MNB.keys(), tag="l0a_gm")
            p.emit("vector", lambda e, o=t["FAR"][:, RSB.lo:RSB.lo + 16],
                   i=t["FAR"][:, RSB.lo:RSB.lo + 16],
                   s=t["FAR"][:, MNB.lo:MNB.lo + 16]:
                   e.scalar_tensor_tensor(o, i, 1.0 / 64, s, ALU.mult, ALU.bypass)
                   if False else
                   e.tensor_scalar_mul(o, i, 1.0 / 64),
                   reads=RSB.keys() + MNB.keys(), writes=RSB.keys(), tag="l0a_gv0")
            p.emit("vector", lambda e, o=sm[:, 64:80], a=t["FAR"][:, MNB.lo:MNB.lo + 16],
                   b=t["FAR"][:, MNB.lo:MNB.lo + 16]:
                   e.tensor_mul(o, a, b),
                   reads=MNB.keys(), writes=[("SMALL", 64 + i) for i in range(16)],
                   tag="l0a_gm2")
            p.emit("vector", lambda e, o=t["FAR"][:, RSB.lo:RSB.lo + 16],
                   a=t["FAR"][:, RSB.lo:RSB.lo + 16], b=sm[:, 64:80]:
                   e.tensor_sub(o, a, b),
                   reads=RSB.keys() + [("SMALL", 64 + i) for i in range(16)],
                   writes=RSB.keys(), tag="l0a_gv")
            p.emit("scalar", lambda e, o=t["FAR"][:, RSB.lo:RSB.lo + 16],
                   i=t["FAR"][:, RSB.lo:RSB.lo + 16]:
                   e.activation(o, i, AF.Ln, bias=float(GN0_EPS)),
                   reads=RSB.keys(), writes=RSB.keys(), tag="l0a_gl")
            p.emit("scalar", lambda e, o=t["FAR"][:, RSB.lo:RSB.lo + 16],
                   i=t["FAR"][:, RSB.lo:RSB.lo + 16]:
                   e.activation(o, i, AF.Exp, scale=-0.5),
                   reads=RSB.keys(), writes=RSB.keys(), tag="l0a_ge")
            p.emit("vector", lambda e, o=Y1.ap(), a=psb[:, 0:1024],
                   b=bass.AP(t["FAR"], MNB.lo, [[FAR_COLS, 128], [1, 16], [0, 64]]):
                   e.tensor_sub(o, a, b),
                   reads=psk("PSB", 0, 1024) + MNB.keys(), writes=Y1.keys(),
                   tag="l0a_yn1")
            p.emit("vector", lambda e, o=Y1.ap(), a=Y1.ap(),
                   b=bass.AP(t["FAR"], RSB.lo, [[FAR_COLS, 128], [1, 16], [0, 64]]):
                   e.tensor_mul(o, a, b),
                   reads=Y1.keys() + RSB.keys(), writes=Y1.keys(), tag="l0a_yn2")
            YNTC = AM
            pk = pst_keys(ctx)
            for cj in range(NCH):
                tr(ctx, pst[:, cj * 128:(cj + 1) * 128],
                   Y1.ap((cj * 128, (cj + 1) * 128)),
                   reads=Y1.keys((cj * 128, (cj + 1) * 128)), writes=pk, tag="l0a_ynt")
            for cj in range(NCH):
                p.emit("vector", lambda e, o=YNTC.ap((cj * 128, (cj + 1) * 128)),
                       i=pst[:, cj * 128:(cj + 1) * 128],
                       s=vec_col(ctx, "L0_attn_ln_w", cj),
                       s2=vec_col(ctx, "L0_attn_ln_b", cj):
                       e.tensor_scalar(o, i, s, s2, ALU.mult, ALU.add),
                       reads=pk + [VEC_KEY],
                       writes=YNTC.keys((cj * 128, (cj + 1) * 128)), tag="l0a_yne")
            if ci == 0 and s2 == 0:
                dbg(ctx, "y1", Y1.ap(), Y1.keys(), cast=True)
                dbg(ctx, "mnb", t["FAR"][:, MNB.lo:MNB.lo + 16], MNB.keys())
                dbg(ctx, "rsb", t["FAR"][:, RSB.lo:RSB.lo + 16], RSB.keys())
            pkb2 = psk("PSB", 0, 1024)
            for nh in range(2):
                for a in range(NCH):
                    mm(ctx, psb[:, nh * 512:(nh + 1) * 512],
                       YNTC.ap((a * 128, (a + 1) * 128)),
                       WO.ap((a * C + nh * 512, a * C + (nh + 1) * 512)),
                       reads=YNTC.keys((a * 128, (a + 1) * 128)) +
                             WO.keys((a * C + nh * 512, a * C + (nh + 1) * 512)),
                       writes=pkb2, start=(a == 0), stop=(a == NCH - 1), tag="l0a_wo")
            stage_epilogue_tile(ctx, ti, t["PSB"], 0, pkb2, xm,
                                Buf(t["FAR"], "FAR", 18456, 18456 + 256) if False else
                                ctx.far(16384, 17408), EPSQ)


STAGE_FNS = {}


def _register_stages():
    STAGE_FNS["l1a"] = stage_l1a
    STAGE_FNS["l0f"] = stage_l0f
    STAGE_FNS["l1f"] = stage_l1f
    STAGE_FNS["l0a"] = stage_l0a


_register_stages()

_BUILD_CACHE = {}


def build_program(stages):
    key = tuple(stages)
    if key in _BUILD_CACHE:
        return _BUILD_CACHE[key]
    stack = contextlib.ExitStack()
    ctx = build_ctx(stack)
    with stack:
        with ctx.nc.Block() as block:
            xm = ctx.far(0, 16384)
            load_commons(ctx)
            load_x(ctx, xm)
            for s in stages:
                STAGE_FNS[s](ctx, xm)
            store_x(ctx, xm)
            ctx.p.finalize(block, ctx.sems)
        nc = ctx.nc
    _BUILD_CACHE[key] = (nc, ctx)
    return nc, ctx


def run_stages(stages, x_full, inputs, aux):
    nc, ctx = build_program(stages)
    names = [n for n in ctx.t
             if isinstance(ctx.t[n], object) and n.startswith(("L0_", "L1_"))]
    in_maps = []
    for core in range(NCORES):
        m = {
            "xin": np.ascontiguousarray(x_full[core]),
            "vecs_pt": aux["vecs_pt"],
            "cbd": _CONSTS["cb"],
            "cfd": _CONSTS["cf"],
        }
        if "w2_lhsT_d" in ctx.t:
            m["w2_lhsT_d"] = aux["w2_lhsT"]
        for n in names:
            m[n] = np.ascontiguousarray(np.asarray(inputs[n], np.float32))
        in_maps.append(m)
    res = run_bass_kernel_spmd(nc, in_maps, core_ids=list(range(NCORES)))
    out = np.stack([res.results[i]["xout"] for i in range(NCORES)])
    if DEBUG:
        global LAST_DBG
        LAST_DBG = (ctx.dbg_map, [res.results[i].get("dbgout") for i in range(NCORES)])
    return out


LAST_DBG = None


STAGE_ORDER = ["l0a", "l0f", "l1a", "l1f"]


def kernel(**inputs):
    inputs.pop("L0_attn_key", None)
    inputs.pop("L0_attn_value", None)
    aux = make_host_aux(inputs)
    x = np.ascontiguousarray(np.asarray(inputs["L0_attn_query"], np.float32))
    return run_stages(STAGE_ORDER, x, inputs, aux)
